# version 10
# baseline (speedup 1.0000x reference)
import numpy as np
import concourse.bass as bass
import concourse.mybir as mybir
from concourse.bass_utils import run_bass_kernel_spmd
from contextlib import ExitStack

F32 = mybir.dt.float32
BF16 = mybir.dt.bfloat16
AF = mybir.ActivationFunctionType
ALU = mybir.AluOpType
AX = mybir.AxisListType

D = 1024
DI = 2048
NS = 16
T = 4096
TT = 512
NTILE = T // TT
NCH = TT // 128
NE = 8
NDC = D // 128
NFC = DI // 128
NEC = DI // 128
EPS = 1e-5
R1024 = 1.0 / 1024.0
EPSK = 1024.0 * EPS
K2 = 1024.0 * EPS * EPS
HL1024 = 3.4657359027997265


def _build():
    nc = bass.Bass("TRN2", target_bir_lowering=False)

    x_in = nc.dram_tensor("x", [T, D], F32, kind="ExternalInput")
    w1_in = nc.dram_tensor("w1", [NE, D, DI], F32, kind="ExternalInput")
    w2_in = nc.dram_tensor("w2", [NE, DI, D], F32, kind="ExternalInput")
    wg_in = nc.dram_tensor("wg", [128, NDC, NE], F32, kind="ExternalInput")
    win_in = nc.dram_tensor("win", [DI, D], F32, kind="ExternalInput")
    ab_in = nc.dram_tensor("ab", [128, NEC, 32], F32, kind="ExternalInput")
    c_in = nc.dram_tensor("cm", [128, NEC, NS], F32, kind="ExternalInput")
    woutt_in = nc.dram_tensor("woutt", [DI, D], F32, kind="ExternalInput")
    bin_in = nc.dram_tensor("binc", [128, NEC], F32, kind="ExternalInput")
    nw_in = nc.dram_tensor("nwc", [128, NDC], F32, kind="ExternalInput")
    nb_in = nc.dram_tensor("nbc", [128, NDC], F32, kind="ExternalInput")
    bout_in = nc.dram_tensor("bout", [1, D], F32, kind="ExternalInput")
    ones_in = nc.dram_tensor("ones", [1, 128], F32, kind="ExternalInput")
    id_in = nc.dram_tensor("ident", [128, 128], F32, kind="ExternalInput")
    out_d = nc.dram_tensor("out", [T, D], F32, kind="ExternalOutput")

    cnt = {}

    def inc(ins, sem, n=1):
        ins.then_inc(sem, n)
        k = id(sem)
        cnt[k] = cnt.get(k, 0) + n
        return cnt[k]

    def val(sem):
        return cnt.get(id(sem), 0)

    with ExitStack() as ctx:
        E = ctx.enter_context
        xnhi_fm = E(nc.sbuf_tensor("xnhi_fm", [128, 2, NDC, TT], BF16, is_transpose_dest=True))
        xnlo_fm = E(nc.sbuf_tensor("xnlo_fm", [128, NDC, 128], BF16, is_transpose_dest=True))
        w1bf = E(nc.sbuf_tensor("w1bf", [128, NDC, DI], BF16))
        w2bf = E(nc.sbuf_tensor("w2bf", [128, NFC, D], BF16))
        h_sb = E(nc.sbuf_tensor("h_sb", [128, NFC, TT], BF16))
        oacc = E(nc.sbuf_tensor("oacc", [128, 2, NCH, D], F32))
        xst = E(nc.sbuf_tensor("xst", [128, 4, D], F32))
        outst = E(nc.sbuf_tensor("outst", [128, 2, D], F32))
        xn_tm = E(nc.sbuf_tensor("xn_tm", [128, D], F32))
        xc_s = E(nc.sbuf_tensor("xc_s", [128, 3, D], F32))
        sq_s = E(nc.sbuf_tensor("sq_s", [128, D], F32))
        xnhi_tm = E(nc.sbuf_tensor("xnhi_tm", [128, D], BF16))
        xnlo_tm = E(nc.sbuf_tensor("xnlo_tm", [128, D], BF16))
        afm = E(nc.sbuf_tensor("afm", [NS, TT], F32))
        ufm = E(nc.sbuf_tensor("ufm", [NS, TT], F32))
        st_f = E(nc.sbuf_tensor("st_f", [NS, TT], F32))
        st_bf = E(nc.sbuf_tensor("st_bf", [NS, 2, TT], BF16))
        carry_s = E(nc.sbuf_tensor("carry_s", [NS, 1], F32))
        au_tm = E(nc.sbuf_tensor("au_tm", [128, 128], F32))
        apre = E(nc.sbuf_tensor("apre", [128, 2, 32], F32))
        term = E(nc.sbuf_tensor("term", [128, 32], F32))
        sn1 = E(nc.sbuf_tensor("sn1", [128, NS], F32))
        sn2 = E(nc.sbuf_tensor("sn2", [128, NS], F32))
        lg_sb = E(nc.sbuf_tensor("lg_sb", [128, 2, NE], F32))
        eq_sb = E(nc.sbuf_tensor("eq_sb", [128, NE], F32))
        msk_sb = E(nc.sbuf_tensor("msk_sb", [128, NE], F32))
        ind_sb = E(nc.sbuf_tensor("ind_sb", [128, NE], F32))
        e_sb = E(nc.sbuf_tensor("e_sb", [128, NE], F32))
        c_sb = E(nc.sbuf_tensor("c_sb", [128, 2, NCH, NE], F32))
        sum_s = E(nc.sbuf_tensor("sum_s", [128, 1], F32))
        mu_s = E(nc.sbuf_tensor("mu_s", [128, 1], F32))
        tq_s = E(nc.sbuf_tensor("tq_s", [128, 1], F32))
        q_s = E(nc.sbuf_tensor("q_s", [128, 1], F32))
        vs_s = E(nc.sbuf_tensor("vs_s", [128, 2], F32))
        sxc_s = E(nc.sbuf_tensor("sxc_s", [128, 4], F32))
        w_s = E(nc.sbuf_tensor("w_s", [128, 2], F32))
        rs_s = E(nc.sbuf_tensor("rs_s", [128, 2], F32))
        u1_s = E(nc.sbuf_tensor("u1_s", [128, 1], F32))
        hu1_s = E(nc.sbuf_tensor("hu1_s", [128, 1], F32))
        u2_s = E(nc.sbuf_tensor("u2_s", [128, 1], F32))
        mu2_s = E(nc.sbuf_tensor("mu2_s", [128, 1], F32))
        rs2_c = E(nc.sbuf_tensor("rs2_c", [128, NCH], F32))
        rm2_c = E(nc.sbuf_tensor("rm2_c", [128, NCH], F32))
        m1_s = E(nc.sbuf_tensor("m1_s", [128, 2], F32))
        nm1_s = E(nc.sbuf_tensor("nm1_s", [128, 2], F32))
        m2_s = E(nc.sbuf_tensor("m2_s", [128, 2], F32))
        e2_s = E(nc.sbuf_tensor("e2_s", [128, 1], F32))
        den_s = E(nc.sbuf_tensor("den_s", [128, 1], F32))
        rden_s = E(nc.sbuf_tensor("rden_s", [128, 1], F32))
        eps_s = E(nc.sbuf_tensor("eps_s", [128, 1], F32))
        epsk_s = E(nc.sbuf_tensor("epsk_s", [128, 1], F32))
        hl_s = E(nc.sbuf_tensor("hl_s", [128, 1], F32))
        wgbf = E(nc.sbuf_tensor("wgbf", [128, NDC, NE], BF16))
        ab_f = E(nc.sbuf_tensor("ab_f", [128, NEC, 32], F32))
        abhi = E(nc.sbuf_tensor("abhi", [128, NEC, 32], BF16))
        ablo = E(nc.sbuf_tensor("ablo", [128, NEC, 32], BF16))
        c_f = E(nc.sbuf_tensor("c_f", [128, NEC, NS], F32))
        chi = E(nc.sbuf_tensor("chi", [128, NEC, NS], BF16))
        clo = E(nc.sbuf_tensor("clo", [128, NEC, NS], BF16))
        bin_f = E(nc.sbuf_tensor("bin_f", [128, NEC], F32))
        binhi = E(nc.sbuf_tensor("binhi", [128, NEC], BF16))
        binlo = E(nc.sbuf_tensor("binlo", [128, NEC], BF16))
        nw_f = E(nc.sbuf_tensor("nw_f", [128, NDC], F32))
        nb_f = E(nc.sbuf_tensor("nb_f", [128, NDC], F32))
        nbhi = E(nc.sbuf_tensor("nbhi", [128, NDC], BF16))
        nblo = E(nc.sbuf_tensor("nblo", [128, NDC], BF16))
        bout_rep = E(nc.sbuf_tensor("bout_rep", [128, D], F32))
        ones_f = E(nc.sbuf_tensor("ones_f", [1, 128], F32))
        id_f = E(nc.sbuf_tensor("id_f", [128, 128], F32))
        onec_bf = E(nc.sbuf_tensor("onec_bf", [128, 1], BF16))
        cab_f = E(nc.sbuf_tensor("cab_f", [128, NDC, 32], F32))
        cabhi = E(nc.sbuf_tensor("cabhi", [128, NDC, 32], BF16))
        cablo = E(nc.sbuf_tensor("cablo", [128, NDC, 32], BF16))
        cabw_f = E(nc.sbuf_tensor("cabw_f", [128, NDC, 32], F32))
        cabwhi = E(nc.sbuf_tensor("cabwhi", [128, NDC, 32], BF16))
        cabwlo = E(nc.sbuf_tensor("cabwlo", [128, NDC, 32], BF16))
        vwrow = E(nc.sbuf_tensor("vwrow", [1, 32], F32))
        cbrow = E(nc.sbuf_tensor("cbrow", [1, 32], F32))
        vwrep = E(nc.sbuf_tensor("vwrep", [128, 32], F32))
        cbrep = E(nc.sbuf_tensor("cbrep", [128, 32], F32))
        mtbf = E(nc.sbuf_tensor("mtbf", [NS, D], BF16))
        ps_h = E(nc.psum_tensor("ps_h", [128, 1024], F32))
        ps_ya = E(nc.psum_tensor("ps_ya", [128, 1024], F32))
        ps_yb = E(nc.psum_tensor("ps_yb", [128, 1024], F32))
        ps_ap = E(nc.psum_tensor("ps_ap", [128, 32], F32))
        ps_misc = E(nc.psum_tensor("ps_misc", [128, 512], F32))
        ps_tr = ps_misc[:, 0:128]
        ps_lg = ps_misc[:, 128:136]
        s_c = E(nc.semaphore("s_c"))
        s_xs = [E(nc.semaphore("s_x%d" % j)) for j in range(4)]
        s_tps = [E(nc.semaphore("s_tp%d" % j)) for j in range(8)]
        s_outs = [E(nc.semaphore("s_out%d" % j)) for j in range(2)]
        s_wg = E(nc.semaphore("s_wg"))
        s_w1 = E(nc.semaphore("s_w1"))
        s_w2 = E(nc.semaphore("s_w2"))
        s_t = E(nc.semaphore("s_t"))
        s_v = E(nc.semaphore("s_v"))
        s_sc = E(nc.semaphore("s_sc"))
        s_ye = E(nc.semaphore("s_ye"))
        s_re = E(nc.semaphore("s_re"))

        M = {}
        LATE = {"lgcp": 0, "aucp": 0, "m6": 0, "apmm": 0, "aumm": 0}
        blk = E(nc.Block())

        @blk.sync
        def _(sy):
            inc(sy.dma_start(out=id_f[:], in_=id_in.ap()), s_c, 16)
            inc(sy.dma_start(out=ones_f[:], in_=ones_in.ap()), s_c, 16)
            inc(sy.dma_start(out=bout_rep[:],
                             in_=bout_in.ap()[0:1, :].partition_broadcast(128)), s_c, 16)
            inc(sy.dma_start(out=nw_f[:], in_=nw_in.ap()), s_c, 16)
            inc(sy.dma_start(out=nb_f[:], in_=nb_in.ap()), s_c, 16)
            inc(sy.dma_start(out=bin_f[:], in_=bin_in.ap()), s_c, 16)
            inc(sy.dma_start(out=ab_f[:], in_=ab_in.ap()), s_c, 16)
            inc(sy.dma_start(out=c_f[:], in_=c_in.ap()), s_c, 16)
            M["consts"] = val(s_c)

        @blk.gpsimd
        def _(g):
            inc(g.dma_start(out=wgbf[:], in_=wg_in.ap()), s_wg, 16)
            w1r = w1_in.ap()[0].rearrange("(dc p) f -> p dc f", p=128)
            inc(g.dma_start(out=w1bf[:, 0:4, :], in_=w1r[:, 0:4, :]), s_w1, 16)
            inc(g.dma_start(out=w1bf[:, 4:8, :], in_=w1r[:, 4:8, :]), s_w1, 16)
            w2r = w2_in.ap()[0].rearrange("(fc p) d -> p fc d", p=128)
            for q in range(4):
                inc(g.dma_start(out=w2bf[:, 4 * q:4 * q + 4, :],
                                in_=w2r[:, 4 * q:4 * q + 4, :]), s_w2, 16)

        @blk.vector
        def _(v):
            v.memset(onec_bf[:], 1.0)
            v.memset(au_tm[:, 16:32], 0.0)
            v.memset(au_tm[:, 48:128], 0.0)
            v.memset(eps_s[:], EPS)
            v.memset(epsk_s[:], EPSK)
            v.memset(hl_s[:], HL1024)

        @blk.scalar
        def _(s):
            s.wait_ge(s_c, M["consts"])
            s.copy(abhi[:], ab_f[:])
            s.copy(chi[:], c_f[:])
            s.copy(binhi[:], bin_f[:])
            inc(s.copy(nbhi[:], nb_f[:]), s_sc)
            M["hi_consts"] = val(s_sc)

        @blk.vector
        def _(v):
            v.wait_ge(s_sc, M["hi_consts"])
            v.tensor_sub(ablo[:], ab_f[:], abhi[:])
            v.tensor_sub(clo[:], c_f[:], chi[:])
            v.tensor_sub(binlo[:], bin_f[:], binhi[:])
            inc(v.tensor_sub(nblo[:], nb_f[:], nbhi[:]), s_v)
            M["lo_consts"] = val(s_v)

        caps = [ps_h[:, 0:32], ps_h[:, 512:544],
                ps_ya[:, 0:32], ps_ya[:, 512:544],
                ps_yb[:, 0:32], ps_yb[:, 512:544],
                ps_ap[:, 0:32], ps_misc[:, 0:32]]
        for ec in range(NEC):
            @blk.sync
            def _(sy, ec=ec):
                if ec >= 2:
                    sy.wait_ge(s_sc, M[("whi", ec - 2)])
                    sy.wait_ge(s_v, M[("wlo", ec - 2)])
                a = sy.dma_start(out=xst[:, ec % 2, :],
                                 in_=win_in.ap()[ec * 128:(ec + 1) * 128, :])
                M[("wld", ec)] = inc(a, s_xs[ec % 2], 16)

            @blk.scalar
            def _(s, ec=ec):
                s.wait_ge(s_xs[ec % 2], M[("wld", ec)])
                if ec >= 1:
                    s.wait_ge(s_t, M[("cabmm", ec - 1)])
                M[("whi", ec)] = inc(s.copy(xnhi_tm[:], xst[:, ec % 2, :]), s_sc)

            @blk.vector
            def _(v, ec=ec):
                v.wait_ge(s_sc, M[("whi", ec)])
                M[("wlo", ec)] = inc(
                    v.tensor_sub(xnlo_tm[:], xst[:, ec % 2, :], xnhi_tm[:]), s_v)

            @blk.tensor
            def _(t, ec=ec):
                t.wait_ge(s_v, M[("wlo", ec)])
                if ec == 0:
                    t.wait_ge(s_sc, M["hi_consts"])
                    t.wait_ge(s_v, M["lo_consts"])
                last = None
                for dc in range(NDC):
                    for p, (lh, rh) in enumerate([
                            (xnhi_tm, abhi), (xnlo_tm, abhi), (xnhi_tm, ablo)]):
                        last = t.matmul(caps[dc], lh[:, dc * 128:(dc + 1) * 128],
                                        rh[:, ec, :],
                                        start=(ec == 0 and p == 0),
                                        stop=(ec == NEC - 1 and p == 2))
                M[("cabmm", ec)] = inc(last, s_t)

        @blk.vector
        def _(v):
            v.wait_ge(s_t, M[("cabmm", NEC - 1)])
            for dc in range(NDC):
                v.tensor_copy(cab_f[:, dc, :], caps[dc])
            v.drain()
            last = None
            for dc in range(NDC):
                last = v.tensor_scalar(cabw_f[:, dc, :], cab_f[:, dc, :],
                                       nw_f[:, dc:dc + 1], None, ALU.mult)
            M["cab_f"] = inc(last, s_v)

        @blk.scalar
        def _(s):
            s.wait_ge(s_v, M["cab_f"])
            s.copy(cabhi[:], cab_f[:])
            M["cabhi"] = inc(s.copy(cabwhi[:], cabw_f[:]), s_sc)

        @blk.vector
        def _(v):
            v.wait_ge(s_sc, M["cabhi"])
            v.tensor_sub(cablo[:], cab_f[:], cabhi[:])
            M["cablo"] = inc(v.tensor_sub(cabwlo[:], cabw_f[:], cabwhi[:]), s_v)

        @blk.tensor
        def _(t):
            t.wait_ge(s_v, M["cablo"])
            for dc in range(NDC):
                for rh in (cabwhi, cabwlo):
                    t.matmul(ps_ap[0:1, :], onec_bf[:], rh[:, dc, :],
                             start=(dc == 0 and rh is cabwhi),
                             stop=(dc == NDC - 1 and rh is cabwlo))
            first = True
            last2 = None
            for dc in range(NDC):
                for lh, rh in [(nbhi, cabhi), (nblo, cabhi), (nbhi, cablo)]:
                    last2 = t.matmul(ps_misc[0:1, 0:32], lh[:, dc:dc + 1],
                                     rh[:, dc, :], start=first, stop=False)
                    first = False
            for ec in range(NEC):
                for p, (lh, rh) in enumerate([(binhi, abhi), (binlo, abhi),
                                              (binhi, ablo)]):
                    last2 = t.matmul(ps_misc[0:1, 0:32], lh[:, ec:ec + 1],
                                     rh[:, ec, :], start=False,
                                     stop=(ec == NEC - 1 and p == 2))
            M["vwcb"] = inc(last2, s_t)

        @blk.vector
        def _(v):
            v.wait_ge(s_t, M["vwcb"])
            v.tensor_copy(vwrow[:], ps_ap[0:1, :])
            M["rows"] = inc(v.tensor_copy(cbrow[:], ps_misc[0:1, 0:32]), s_v)

        @blk.tensor
        def _(t):
            t.wait_ge(s_v, M["rows"])
            M["rep1"] = inc(t.matmul(ps_ap[:], ones_f[:], vwrow[:],
                                     start=True, stop=True), s_t)

        @blk.vector
        def _(v):
            v.wait_ge(s_t, M["rep1"])
            M["vwrep"] = inc(v.tensor_copy(vwrep[:], ps_ap[:]), s_v)

        @blk.tensor
        def _(t):
            t.wait_ge(s_v, M["vwrep"])
            M["rep2"] = inc(t.matmul(ps_ap[:], ones_f[:], cbrow[:],
                                     start=True, stop=True), s_t)

        @blk.vector
        def _(v):
            v.wait_ge(s_t, M["rep2"])
            M["cbrep"] = inc(v.tensor_copy(cbrep[:], ps_ap[:]), s_v)

        for ec in range(NEC):
            @blk.sync
            def _(sy, ec=ec):
                sy.wait_ge(s_sc, M[("mwhi", ec - 2)] if ec >= 2 else M[("whi", NEC - 1)])
                sy.wait_ge(s_v, M[("mwlo", ec - 2)] if ec >= 2 else M[("wlo", NEC - 1)])
                a = sy.dma_start(out=xst[:, ec % 2, :],
                                 in_=woutt_in.ap()[ec * 128:(ec + 1) * 128, :])
                M[("mwld", ec)] = inc(a, s_xs[ec % 2], 16)

            @blk.scalar
            def _(s, ec=ec):
                s.wait_ge(s_xs[ec % 2], M[("mwld", ec)])
                s.wait_ge(s_t, M[("mtmm", ec - 1)] if ec >= 1 else M[("cabmm", NEC - 1)])
                M[("mwhi", ec)] = inc(s.copy(xnhi_tm[:], xst[:, ec % 2, :]), s_sc)

            @blk.vector
            def _(v, ec=ec):
                v.wait_ge(s_sc, M[("mwhi", ec)])
                M[("mwlo", ec)] = inc(
                    v.tensor_sub(xnlo_tm[:], xst[:, ec % 2, :], xnhi_tm[:]), s_v)

            @blk.tensor
            def _(t, ec=ec):
                t.wait_ge(s_v, M[("mwlo", ec)])
                if ec == 0:
                    t.wait_ge(s_v, M["cbrep"])
                last = None
                for hf in range(2):
                    sl = slice(hf * 512, hf * 512 + 512)
                    for p, (lh, rh) in enumerate([(chi, xnhi_tm), (clo, xnhi_tm),
                                                  (chi, xnlo_tm)]):
                        last = t.matmul(ps_yb[0:NS, sl], lh[:, ec, :], rh[:, sl],
                                        start=(ec == 0 and p == 0),
                                        stop=(ec == NEC - 1 and p == 2))
                M[("mtmm", ec)] = inc(last, s_t)

        @blk.vector
        def _(v):
            v.wait_ge(s_t, M[("mtmm", NEC - 1)])
            M["mtbf"] = inc(v.tensor_copy(mtbf[:], ps_yb[0:NS, :]), s_v)

        ge_w1free = {}
        ge_w2free = {}
        xfree = {}
        stores = [0]

        def s1VA(ti, c):
            g = ti * NCH + c
            gp = g % 2
            g3 = g % 3
            g4 = g % 4
            xs = g % 4
            t0 = g * 128

            @blk.sync
            def _(sy):
                if g >= 4:
                    sy.wait_ge(s_v, xfree[g - 4])
                else:
                    sy.wait_ge(s_sc, M[("mwhi", NEC - 1)])
                    sy.wait_ge(s_v, M[("mwlo", NEC - 1)])
                a = sy.dma_start(out=xst[:, xs, :], in_=x_in.ap()[t0:t0 + 128, :])
                M[("x", g)] = inc(a, s_xs[xs], 16)

            @blk.vector
            def _(v):
                v.wait_ge(s_xs[xs], M[("x", g)])
                xt = xst[:, xs, :]
                v.reduce_sum(sum_s[:], xt, AX.X)
                v.drain()
                v.tensor_scalar_mul(mu_s[:], sum_s[:], R1024)
                v.drain()
                v.tensor_scalar_sub(xc_s[:, g3, :], xt, mu_s[:])
                v.drain()
                v.tensor_mul(sq_s[:], xc_s[:, g3, :], xc_s[:, g3, :])
                v.reduce_sum(sxc_s[:, g4:g4 + 1], xc_s[:, g3, :], AX.X)
                v.drain()
                v.reduce_sum(vs_s[:, gp:gp + 1], sq_s[:], AX.X)
                v.tensor_mul(tq_s[:], sxc_s[:, g4:g4 + 1], sxc_s[:, g4:g4 + 1])
                v.drain()
                v.scalar_tensor_tensor(q_s[:], tq_s[:], -R1024, vs_s[:, gp:gp + 1],
                                       ALU.mult, ALU.add)
                v.drain()
                v.scalar_tensor_tensor(w_s[:, gp:gp + 1], vs_s[:, gp:gp + 1], EPS,
                                       q_s[:], ALU.mult, ALU.add)
                v.drain()
                M[("m1", g)] = inc(
                    v.tensor_scalar_add(w_s[:, gp:gp + 1], w_s[:, gp:gp + 1], K2),
                    s_v)

        def s1SB(ti, c):
            g = ti * NCH + c
            gp = g % 2

            @blk.scalar
            def _(s):
                s.wait_ge(s_v, M[("m1", g)])
                s.activation(u1_s[:], vs_s[:, gp:gp + 1], AF.Ln, bias=epsk_s[:])
                s.drain()
                s.mul(hu1_s[:], u1_s[:], 0.5)
                s.activation(rs_s[:, gp:gp + 1], u1_s[:], AF.Exp,
                             scale=-0.5, bias=hl_s[:])
                s.drain()
                s.activation(u2_s[:], w_s[:, gp:gp + 1], AF.Ln)
                s.drain()
                M[("m2", g)] = inc(
                    s.activation(rs2_c[:, c:c + 1], u2_s[:], AF.Exp,
                                 scale=-0.5, bias=hu1_s[:]), s_sc)

        def s1VC(ti, c):
            g = ti * NCH + c
            gp = g % 2
            g3 = g % 3
            g4 = g % 4
            xs = g % 4
            tb = ti % 2

            @blk.vector
            def _(v):
                v.wait_ge(s_sc, M[("m2", g)])
                if g >= 1:
                    v.wait_ge(s_tps[(g - 1) % 8], M[("tpd", g - 1)])
                else:
                    v.wait_ge(s_t, M[("mtmm", NEC - 1)])
                    v.wait_ge(s_c, M["consts"])
                v.tensor_scalar_mul(xn_tm[:], xc_s[:, g3, :], rs_s[:, gp:gp + 1])
                v.drain()
                v.tensor_copy(xnhi_tm[:], xn_tm[:])
                v.drain()
                v.tensor_sub(xnlo_tm[:], xn_tm[:], xnhi_tm[:])
                v.tensor_scalar_mul(mu2_s[:], sxc_s[:, g4:g4 + 1], R1024)
                v.drain()
                v.tensor_mul(mu2_s[:], mu2_s[:], rs_s[:, gp:gp + 1])
                v.drain()
                v.tensor_mul(rm2_c[:, c:c + 1], mu2_s[:], rs2_c[:, c:c + 1])
                v.tensor_add(oacc[:, tb, c, :], xst[:, xs, :], xn_tm[:])
                v.drain()
                ii = v.tensor_add(oacc[:, tb, c, :], oacc[:, tb, c, :], bout_rep[:])
                xfree[g] = inc(ii, s_v)

            @blk.sync
            def _(sy):
                if c == 0 and ti >= 2:
                    sy.wait_ge(s_t, ge_w1free[(ti - 1) * NE])
                sy.wait_ge(s_v, xfree[g])
                stp = s_tps[g % 8]
                for dc in range(NDC):
                    inc(sy.dma_start_transpose(
                        xnhi_fm[:, tb, dc, c * 128:(c + 1) * 128],
                        xnhi_tm[:, dc * 128:(dc + 1) * 128]), stp, 16)
                if g >= 1:
                    sy.wait_ge(s_t, LATE["apmm"])
                for dc in range(NDC):
                    inc(sy.dma_start_transpose(
                        xnlo_fm[:, dc, :],
                        xnlo_tm[:, dc * 128:(dc + 1) * 128]), stp, 16)
                M[("tpd", g)] = val(stp)

        def s1B1(ti, c):
            g = ti * NCH + c
            tb = ti % 2
            pc = c % 2

            @blk.tensor
            def _(t):
                if g == 0:
                    t.wait_ge(s_wg, 16)
                    t.wait_ge(s_sc, M["cabhi"])
                    t.wait_ge(s_v, M["cablo"])
                    t.wait_ge(s_v, M["mtbf"])
                    t.wait_ge(s_v, M["cbrep"])
                if LATE["aucp"]:
                    t.wait_ge(s_v, LATE["aucp"])
                if LATE["lgcp"]:
                    t.wait_ge(s_v, LATE["lgcp"])
                t.wait_ge(s_tps[g % 8], M[("tpd", g)])
                last = None
                for dc in range(NDC):
                    last = t.matmul(ps_lg, xnhi_fm[:, tb, dc, c * 128:(c + 1) * 128],
                                    wgbf[:, dc, :],
                                    start=(dc == 0), stop=(dc == NDC - 1))
                M[("lgmm", g)] = inc(last, s_t)
                if LATE["m6"]:
                    t.wait_ge(s_v, LATE["m6"])
                for p in range(3):
                    for dc in range(NDC):
                        lh = (xnhi_fm[:, tb, dc, c * 128:(c + 1) * 128]
                              if p != 1 else xnlo_fm[:, dc, :])
                        rh = cabwhi[:, dc, :] if p != 2 else cabwlo[:, dc, :]
                        last = t.matmul(ps_ap[:], lh, rh,
                                        start=(p == 0 and dc == 0),
                                        stop=(p == 2 and dc == NDC - 1))
                M[("apmm", g)] = inc(last, s_t)
                LATE["apmm"] = M[("apmm", g)]

            @blk.vector
            def _(v):
                if g >= 2:
                    v.wait_ge(s_sc, M[("m10", g - 2)])
                v.wait_ge(s_t, M[("lgmm", g)])
                M[("lgcp", g)] = inc(v.tensor_copy(lg_sb[:, pc, :], ps_lg), s_v)
                LATE["lgcp"] = M[("lgcp", g)]
                v.wait_ge(s_t, M[("apmm", g)])
                v.scalar_tensor_tensor(term[:], vwrep[:], rm2_c[:, c:c + 1],
                                       cbrep[:], ALU.mult, ALU.subtract)
                v.drain()
                v.scalar_tensor_tensor(apre[:, pc, :], ps_ap[:],
                                        rs2_c[:, c:c + 1], term[:],
                                        ALU.mult, ALU.subtract)
                v.drain()
                M[("m6", g)] = inc(
                    v.tensor_scalar_max(apre[:, pc, 0:NS], apre[:, pc, 0:NS],
                                        -30.0), s_v)
                LATE["m6"] = M[("m6", g)]
                v.reduce_max(m1_s[:, pc:pc + 1], lg_sb[:, pc, :], AX.X)
                v.reduce_max(nm1_s[:, pc:pc + 1], lg_sb[:, pc, :], AX.X, negate=True)
                v.drain()
                v.tensor_scalar(eq_sb[:], lg_sb[:, pc, :], m1_s[:, pc:pc + 1],
                                None, ALU.is_equal)
                v.drain()
                v.scalar_tensor_tensor(msk_sb[:], eq_sb[:], -1e30, lg_sb[:, pc, :],
                                       ALU.mult, ALU.add)
                v.drain()
                M[("m9", g)] = inc(
                    v.reduce_max(m2_s[:, pc:pc + 1], msk_sb[:], AX.X), s_v)

        def s1B2(ti, c):
            g = ti * NCH + c
            tb = ti % 2
            pc = c % 2

            @blk.scalar
            def _(s):
                if g >= 1:
                    s.wait_ge(s_v, M[("router", g - 1)])
                s.wait_ge(s_v, M[("m6", g)])
                M[("m7", g)] = inc(
                    s.activation(sn1[:], apre[:, pc, 0:NS], AF.Exp, scale=-1.0),
                    s_sc)
                s.wait_ge(s_v, M[("m9", g)])
                s.activation(e_sb[:], lg_sb[:, pc, :], AF.Exp,
                             bias=nm1_s[:, pc:pc + 1])
                M[("m10", g)] = inc(
                    s.activation(e2_s[:], m2_s[:, pc:pc + 1], AF.Exp,
                                 bias=nm1_s[:, pc:pc + 1]), s_sc)

            @blk.vector
            def _(v):
                v.wait_ge(s_sc, M[("m7", g)])
                if LATE["aumm"]:
                    v.wait_ge(s_t, LATE["aumm"])
                v.tensor_copy(au_tm[:, 32:48], apre[:, pc, 16:32])
                v.tensor_scalar_add(sn2[:], sn1[:], 1.0)
                v.drain()
                M[("m8", g)] = inc(v.reciprocal(au_tm[:, 0:16], sn2[:]), s_v)
                v.wait_ge(s_sc, M[("m10", g)])
                v.tensor_scalar_add(den_s[:], e2_s[:], 1.0)
                v.drain()
                v.reciprocal(rden_s[:], den_s[:])
                v.tensor_scalar(ind_sb[:], lg_sb[:, pc, :], m2_s[:, pc:pc + 1],
                                None, ALU.is_ge)
                v.drain()
                M[("router", g)] = inc(
                    v.scalar_tensor_tensor(c_sb[:, tb, c, :], ind_sb[:], rden_s[:],
                                           e_sb[:], ALU.mult, ALU.mult), s_v)

            @blk.tensor
            def _(t):
                t.wait_ge(s_v, LATE["lgcp"])
                t.wait_ge(s_v, M[("m8", g)])
                if g == 0:
                    t.wait_ge(s_c, M["consts"])
                M[("aumm", g)] = inc(t.transpose(ps_tr, au_tm[:], id_f[:]), s_t)
                LATE["aumm"] = M[("aumm", g)]

            @blk.vector
            def _(v):
                v.wait_ge(s_t, M[("aumm", g)])
                v.tensor_copy(afm[:, c * 128:(c + 1) * 128], ps_misc[0:NS, 0:128])
                ii = v.tensor_copy(ufm[:, c * 128:(c + 1) * 128],
                                   ps_misc[32:48, 0:128])
                M[("aucp", g)] = inc(ii, s_v)
                LATE["aucp"] = M[("aucp", g)]
                v.drain()

        def scan_emit(ti):
            tb = ti % 2

            @blk.vector
            def _(v):
                if ti % 4 == 0:
                    v.memset(carry_s[:], 0.0)
                v.drain()
                v.tensor_tensor_scan(st_f[:], afm[:], ufm[:],
                                     carry_s[:], ALU.mult, ALU.add)
                v.drain()
                v.tensor_copy(carry_s[:], st_f[:, TT - 1:TT])
                M[("scan", ti)] = inc(v.tensor_copy(st_bf[:, tb, :], st_f[:]), s_v)

        def expert_emit(ti, e):
            ge = ti * NE + e
            tb = ti % 2

            @blk.tensor
            def _(t):
                t.wait_ge(s_w1, 32 * (ge + 1))
                if e == 0:
                    for cc in range(NCH):
                        gg = ti * NCH + cc
                        t.wait_ge(s_tps[gg % 8], M[("tpd", gg)])
                for fc in range(NFC):
                    pp = (fc % 2) * 512
                    rk = 16 * ge + fc - 1
                    if rk > 0:
                        t.wait_ge(s_re, rk)
                    for dc in range(NDC):
                        last = t.matmul(ps_h[:, pp:pp + 512],
                                        w1bf[:, dc, fc * 128:(fc + 1) * 128],
                                        xnhi_fm[:, tb, dc, :],
                                        start=(dc == 0), stop=(dc == NDC - 1))
                    M[("hps", ge, fc)] = inc(last, s_t)
                ge_w1free[ge + 1] = val(s_t)
                t.wait_ge(s_w2, 64 * (ge + 1))
                t.wait_ge(s_re, 16 * ge + 16)
                if e == 0:
                    t.wait_ge(s_v, val(s_v))
                for tc in range(NCH):
                    if 4 * ge + tc - 1 > 0:
                        t.wait_ge(s_ye, 4 * ge + tc - 1)
                    ps = ps_ya if tc % 2 == 0 else ps_yb
                    for hf in range(2):
                        sl = slice(hf * 512, hf * 512 + 512)
                        for fc in range(NFC):
                            last = t.matmul(ps[:, sl],
                                            h_sb[:, fc, tc * 128:(tc + 1) * 128],
                                            w2bf[:, fc, sl],
                                            start=(fc == 0), stop=(fc == NFC - 1))
                    M[("yps", ge, tc)] = inc(last, s_t)
                ge_w2free[ge + 1] = val(s_t)

            @blk.scalar
            def _(s):
                for fc in range(NFC):
                    s.wait_ge(s_t, M[("hps", ge, fc)])
                    pp = (fc % 2) * 512
                    inc(s.activation(h_sb[:, fc, :], ps_h[:, pp:pp + 512],
                                     AF.Relu), s_re)

            @blk.vector
            def _(v):
                for tc in range(NCH):
                    v.wait_ge(s_t, M[("yps", ge, tc)])
                    ps = ps_ya if tc % 2 == 0 else ps_yb
                    inc(v.scalar_tensor_tensor(
                        oacc[:, tb, tc, :], ps[:], c_sb[:, tb, tc, e:e + 1],
                        oacc[:, tb, tc, :], ALU.mult, ALU.add), s_ye)

            @blk.gpsimd
            def _(g):
                nge = ge + 1
                if nge >= NTILE * NE:
                    return
                ne_ = nge % NE
                g.wait_ge(s_t, ge_w1free[nge])
                w1r = w1_in.ap()[ne_].rearrange("(dc p) f -> p dc f", p=128)
                inc(g.dma_start(out=w1bf[:, 0:4, :], in_=w1r[:, 0:4, :]), s_w1, 16)
                inc(g.dma_start(out=w1bf[:, 4:8, :], in_=w1r[:, 4:8, :]), s_w1, 16)
                g.wait_ge(s_t, ge_w2free[nge])
                w2r = w2_in.ap()[ne_].rearrange("(fc p) d -> p fc d", p=128)
                for q in range(4):
                    inc(g.dma_start(out=w2bf[:, 4 * q:4 * q + 4, :],
                                    in_=w2r[:, 4 * q:4 * q + 4, :]), s_w2, 16)

        def final_emit(ti):
            tb = ti % 2
            toff = ti * TT
            for c in range(NCH):
                t0 = toff + c * 128

                @blk.tensor
                def _(t, c=c, t0=t0):
                    t.wait_ge(s_v, M[("scan", ti)])
                    if c < 2:
                        t.wait_ge(s_ye, val(s_ye))
                    else:
                        t.wait_ge(s_v, M[("fin", ti, c - 2)])
                    ps = ps_ya if c % 2 == 0 else ps_yb
                    last = None
                    for hf in range(2):
                        sl = slice(hf * 512, hf * 512 + 512)
                        last = t.matmul(ps[:, sl],
                                        st_bf[:, tb, c * 128:(c + 1) * 128],
                                        mtbf[:, sl], start=True, stop=True)
                    M[("ssm", ti, c)] = inc(last, s_t)

                @blk.vector
                def _(v, c=c):
                    v.wait_ge(s_t, M[("ssm", ti, c)])
                    if ("store", c % 2) in M:
                        v.wait_ge(s_outs[c % 2], M[("store", c % 2)])
                    v.drain()
                    ps = ps_ya if c % 2 == 0 else ps_yb
                    M[("fin", ti, c)] = inc(
                        v.tensor_add(outst[:, c % 2, :], oacc[:, tb, c, :], ps[:]),
                        s_v)

                @blk.sync
                def _(sy, c=c, t0=t0):
                    sy.wait_ge(s_v, M[("fin", ti, c)])
                    M[("store", c % 2)] = inc(
                        sy.dma_start(out=out_d.ap()[t0:t0 + 128, :],
                                     in_=outst[:, c % 2, :]), s_outs[c % 2], 16)
                stores[0] += 1

        for c in range(NCH):
            s1VA(0, c)
            s1SB(0, c)
            s1VC(0, c)
            s1B1(0, c)
            s1B2(0, c)
        scan_emit(0)
        for ti in range(NTILE):
            for e in range(NE):
                expert_emit(ti, e)
                if ti + 1 < NTILE:
                    if 3 <= e <= 6:
                        s1B1(ti + 1, e - 3)
                    if 4 <= e <= 7:
                        s1B2(ti + 1, e - 4)
                    if e <= 3:
                        s1VA(ti + 1, e)
                    if 1 <= e <= 4:
                        s1SB(ti + 1, e - 1)
                    if 2 <= e <= 5:
                        s1VC(ti + 1, e - 2)
                    if e == 7:
                        scan_emit(ti + 1)
            final_emit(ti)

        @blk.sync
        def _(sy):
            sy.wait_ge(s_outs[0], val(s_outs[0]))
            sy.wait_ge(s_outs[1], val(s_outs[1]))

    return nc


_NC_CACHE = [None]


def kernel(**inputs):
    x = np.ascontiguousarray(np.asarray(inputs["L0_attn_x"], dtype=np.float32))
    W1 = np.ascontiguousarray(np.asarray(inputs["L0_ffn_W1"], dtype=np.float32))
    W2 = np.ascontiguousarray(np.asarray(inputs["L0_ffn_W2"], dtype=np.float32))
    Wg = np.asarray(inputs["L0_ffn_Wg"], dtype=np.float32)
    W_in = np.ascontiguousarray(np.asarray(inputs["L0_attn_W_in"], dtype=np.float32))
    A = np.asarray(inputs["L0_attn_A"], dtype=np.float32)
    B = np.asarray(inputs["L0_attn_B_ssm"], dtype=np.float32)
    C = np.asarray(inputs["L0_attn_C_ssm"], dtype=np.float32)
    W_out = np.asarray(inputs["L0_attn_W_out"], dtype=np.float32)
    b_in = np.asarray(inputs["L0_attn_b_in"], dtype=np.float32)
    b_out = np.asarray(inputs["L0_attn_b_out"], dtype=np.float32)
    norm_w = np.asarray(inputs["L0_attn_norm_w"], dtype=np.float32)
    norm_b = np.asarray(inputs["L0_attn_norm_b"], dtype=np.float32)

    AB = np.concatenate([A, B], axis=1)
    ab_l = np.ascontiguousarray(AB.reshape(16, 128, 32).transpose(1, 0, 2))
    c_l = np.ascontiguousarray(C.reshape(16, 128, 16).transpose(1, 0, 2))
    woutt = np.ascontiguousarray(W_out.T)
    wg_l = np.ascontiguousarray(Wg.reshape(8, 128, 8).transpose(1, 0, 2))
    bin_l = np.ascontiguousarray(b_in.reshape(16, 128).T)
    nw_l = np.ascontiguousarray(norm_w.reshape(8, 128).T)
    nb_l = np.ascontiguousarray(norm_b.reshape(8, 128).T)
    bout_l = np.ascontiguousarray(b_out.reshape(1, 1024))
    ones = np.ones((1, 128), np.float32)
    ident = np.eye(128, dtype=np.float32)

    if _NC_CACHE[0] is None:
        _NC_CACHE[0] = _build()
    nc = _NC_CACHE[0]

    in_maps = []
    for i in range(8):
        xi = np.ascontiguousarray(x[2 * i:2 * i + 2].reshape(T, D))
        in_maps.append({
            "x": xi, "w1": W1, "w2": W2, "wg": wg_l, "win": W_in,
            "ab": ab_l, "cm": c_l, "woutt": woutt, "binc": bin_l,
            "nwc": nw_l, "nbc": nb_l, "bout": bout_l, "ones": ones,
            "ident": ident,
        })
    res = run_bass_kernel_spmd(nc, in_maps, core_ids=list(range(8)))
    out = np.stack([np.asarray(r["out"]) for r in res.results])
    return np.ascontiguousarray(out.reshape(16, 2048, 1024)).astype(np.float32, copy=False)


# version 11
# speedup vs baseline: 1.0094x; 1.0094x over previous
import numpy as np
import concourse.bass as bass
import concourse.mybir as mybir
from concourse.bass_utils import run_bass_kernel_spmd
from contextlib import ExitStack

F32 = mybir.dt.float32
BF16 = mybir.dt.bfloat16
AF = mybir.ActivationFunctionType
ALU = mybir.AluOpType
AX = mybir.AxisListType

D = 1024
DI = 2048
NS = 16
T = 4096
TT = 512
NTILE = T // TT
NCH = TT // 128
NE = 8
NDC = D // 128
NFC = DI // 128
NEC = DI // 128
EPS = 1e-5
R1024 = 1.0 / 1024.0
EPSK = 1024.0 * EPS
K2 = 1024.0 * EPS * EPS
HL1024 = 3.4657359027997265


def _build():
    nc = bass.Bass("TRN2", target_bir_lowering=False)

    x_in = nc.dram_tensor("x", [T, D], F32, kind="ExternalInput")
    w1_in = nc.dram_tensor("w1", [NE, D, DI], F32, kind="ExternalInput")
    w2_in = nc.dram_tensor("w2", [NE, DI, D], F32, kind="ExternalInput")
    wg_in = nc.dram_tensor("wg", [128, NDC, NE], F32, kind="ExternalInput")
    win_in = nc.dram_tensor("win", [DI, D], F32, kind="ExternalInput")
    ab_in = nc.dram_tensor("ab", [128, NEC, 32], F32, kind="ExternalInput")
    c_in = nc.dram_tensor("cm", [128, NEC, NS], F32, kind="ExternalInput")
    woutt_in = nc.dram_tensor("woutt", [DI, D], F32, kind="ExternalInput")
    bin_in = nc.dram_tensor("binc", [128, NEC], F32, kind="ExternalInput")
    nw_in = nc.dram_tensor("nwc", [128, NDC], F32, kind="ExternalInput")
    nb_in = nc.dram_tensor("nbc", [128, NDC], F32, kind="ExternalInput")
    bout_in = nc.dram_tensor("bout", [1, D], F32, kind="ExternalInput")
    ones_in = nc.dram_tensor("ones", [1, 128], F32, kind="ExternalInput")
    id_in = nc.dram_tensor("ident", [128, 128], F32, kind="ExternalInput")
    out_d = nc.dram_tensor("out", [T, D], F32, kind="ExternalOutput")

    cnt = {}

    def inc(ins, sem, n=1):
        ins.then_inc(sem, n)
        k = id(sem)
        cnt[k] = cnt.get(k, 0) + n
        return cnt[k]

    def val(sem):
        return cnt.get(id(sem), 0)

    with ExitStack() as ctx:
        E = ctx.enter_context
        xnhi_fm = E(nc.sbuf_tensor("xnhi_fm", [128, 2, NDC, TT], BF16, is_transpose_dest=True))
        xnlo_fm = E(nc.sbuf_tensor("xnlo_fm", [128, NDC, 128], BF16, is_transpose_dest=True))
        w1bf = E(nc.sbuf_tensor("w1bf", [128, NDC, DI], BF16))
        w2bf = E(nc.sbuf_tensor("w2bf", [128, NFC, D], BF16))
        h_sb = E(nc.sbuf_tensor("h_sb", [128, NFC, TT], BF16))
        oacc = E(nc.sbuf_tensor("oacc", [128, 2, NCH, D], F32))
        xst = E(nc.sbuf_tensor("xst", [128, 4, D], F32))
        outst = E(nc.sbuf_tensor("outst", [128, 2, D], F32))
        xn_tm = E(nc.sbuf_tensor("xn_tm", [128, D], F32))
        xc_s = E(nc.sbuf_tensor("xc_s", [128, 3, D], F32))
        sq_s = E(nc.sbuf_tensor("sq_s", [128, D], F32))
        xnhi_tm = E(nc.sbuf_tensor("xnhi_tm", [128, D], BF16))
        xnlo_tm = E(nc.sbuf_tensor("xnlo_tm", [128, D], BF16))
        afm = E(nc.sbuf_tensor("afm", [NS, TT], F32))
        ufm = E(nc.sbuf_tensor("ufm", [NS, TT], F32))
        st_f = E(nc.sbuf_tensor("st_f", [NS, TT], F32))
        st_bf = E(nc.sbuf_tensor("st_bf", [NS, 2, TT], BF16))
        carry_s = E(nc.sbuf_tensor("carry_s", [NS, 1], F32))
        au_tm = E(nc.sbuf_tensor("au_tm", [128, 128], F32))
        apre = E(nc.sbuf_tensor("apre", [128, 2, 32], F32))
        term = E(nc.sbuf_tensor("term", [128, 32], F32))
        sn1 = E(nc.sbuf_tensor("sn1", [128, NS], F32))
        sn2 = E(nc.sbuf_tensor("sn2", [128, NS], F32))
        lg_sb = E(nc.sbuf_tensor("lg_sb", [128, 2, NE], F32))
        eq_sb = E(nc.sbuf_tensor("eq_sb", [128, NE], F32))
        msk_sb = E(nc.sbuf_tensor("msk_sb", [128, NE], F32))
        ind_sb = E(nc.sbuf_tensor("ind_sb", [128, NE], F32))
        e_sb = E(nc.sbuf_tensor("e_sb", [128, NE], F32))
        c_sb = E(nc.sbuf_tensor("c_sb", [128, 2, NCH, NE], F32))
        sum_s = E(nc.sbuf_tensor("sum_s", [128, 1], F32))
        mu_s = E(nc.sbuf_tensor("mu_s", [128, 1], F32))
        tq_s = E(nc.sbuf_tensor("tq_s", [128, 1], F32))
        q_s = E(nc.sbuf_tensor("q_s", [128, 1], F32))
        vs_s = E(nc.sbuf_tensor("vs_s", [128, 3], F32))
        sxc_s = E(nc.sbuf_tensor("sxc_s", [128, 4], F32))
        w_s = E(nc.sbuf_tensor("w_s", [128, 3], F32))
        rs_s = E(nc.sbuf_tensor("rs_s", [128, 3], F32))
        u1_s = E(nc.sbuf_tensor("u1_s", [128, 1], F32))
        hu1_s = E(nc.sbuf_tensor("hu1_s", [128, 1], F32))
        u2_s = E(nc.sbuf_tensor("u2_s", [128, 1], F32))
        mu2_s = E(nc.sbuf_tensor("mu2_s", [128, 1], F32))
        rs2_c = E(nc.sbuf_tensor("rs2_c", [128, NCH], F32))
        rm2_c = E(nc.sbuf_tensor("rm2_c", [128, NCH], F32))
        m1_s = E(nc.sbuf_tensor("m1_s", [128, 2], F32))
        nm1_s = E(nc.sbuf_tensor("nm1_s", [128, 2], F32))
        m2_s = E(nc.sbuf_tensor("m2_s", [128, 2], F32))
        e2_s = E(nc.sbuf_tensor("e2_s", [128, 1], F32))
        den_s = E(nc.sbuf_tensor("den_s", [128, 1], F32))
        rden_s = E(nc.sbuf_tensor("rden_s", [128, 1], F32))
        eps_s = E(nc.sbuf_tensor("eps_s", [128, 1], F32))
        epsk_s = E(nc.sbuf_tensor("epsk_s", [128, 1], F32))
        hl_s = E(nc.sbuf_tensor("hl_s", [128, 1], F32))
        wgbf = E(nc.sbuf_tensor("wgbf", [128, NDC, NE], BF16))
        ab_f = E(nc.sbuf_tensor("ab_f", [128, NEC, 32], F32))
        abhi = E(nc.sbuf_tensor("abhi", [128, NEC, 32], BF16))
        ablo = E(nc.sbuf_tensor("ablo", [128, NEC, 32], BF16))
        c_f = E(nc.sbuf_tensor("c_f", [128, NEC, NS], F32))
        chi = E(nc.sbuf_tensor("chi", [128, NEC, NS], BF16))
        clo = E(nc.sbuf_tensor("clo", [128, NEC, NS], BF16))
        bin_f = E(nc.sbuf_tensor("bin_f", [128, NEC], F32))
        binhi = E(nc.sbuf_tensor("binhi", [128, NEC], BF16))
        binlo = E(nc.sbuf_tensor("binlo", [128, NEC], BF16))
        nw_f = E(nc.sbuf_tensor("nw_f", [128, NDC], F32))
        nb_f = E(nc.sbuf_tensor("nb_f", [128, NDC], F32))
        nbhi = E(nc.sbuf_tensor("nbhi", [128, NDC], BF16))
        nblo = E(nc.sbuf_tensor("nblo", [128, NDC], BF16))
        bout_rep = E(nc.sbuf_tensor("bout_rep", [128, D], F32))
        ones_f = E(nc.sbuf_tensor("ones_f", [1, 128], F32))
        id_f = E(nc.sbuf_tensor("id_f", [128, 128], F32))
        onec_bf = E(nc.sbuf_tensor("onec_bf", [128, 1], BF16))
        cab_f = E(nc.sbuf_tensor("cab_f", [128, NDC, 32], F32))
        cabhi = E(nc.sbuf_tensor("cabhi", [128, NDC, 32], BF16))
        cablo = E(nc.sbuf_tensor("cablo", [128, NDC, 32], BF16))
        cabw_f = E(nc.sbuf_tensor("cabw_f", [128, NDC, 32], F32))
        cabwhi = E(nc.sbuf_tensor("cabwhi", [128, NDC, 32], BF16))
        cabwlo = E(nc.sbuf_tensor("cabwlo", [128, NDC, 32], BF16))
        vwrow = E(nc.sbuf_tensor("vwrow", [1, 32], F32))
        cbrow = E(nc.sbuf_tensor("cbrow", [1, 32], F32))
        vwrep = E(nc.sbuf_tensor("vwrep", [128, 32], F32))
        cbrep = E(nc.sbuf_tensor("cbrep", [128, 32], F32))
        mtbf = E(nc.sbuf_tensor("mtbf", [NS, D], BF16))
        ps_h = E(nc.psum_tensor("ps_h", [128, 1024], F32))
        ps_ya = E(nc.psum_tensor("ps_ya", [128, 1024], F32))
        ps_yb = E(nc.psum_tensor("ps_yb", [128, 1024], F32))
        ps_ap = E(nc.psum_tensor("ps_ap", [128, 32], F32))
        ps_misc = E(nc.psum_tensor("ps_misc", [128, 512], F32))
        ps_tr = ps_misc[:, 0:128]
        ps_lg = ps_misc[:, 128:136]
        s_c = E(nc.semaphore("s_c"))
        s_xs = [E(nc.semaphore("s_x%d" % j)) for j in range(4)]
        s_tps = [E(nc.semaphore("s_tp%d" % j)) for j in range(8)]
        s_outs = [E(nc.semaphore("s_out%d" % j)) for j in range(2)]
        s_wg = E(nc.semaphore("s_wg"))
        s_w1 = E(nc.semaphore("s_w1"))
        s_w2 = E(nc.semaphore("s_w2"))
        s_t = E(nc.semaphore("s_t"))
        s_v = E(nc.semaphore("s_v"))
        s_sc = E(nc.semaphore("s_sc"))
        s_ye = E(nc.semaphore("s_ye"))
        s_re = E(nc.semaphore("s_re"))

        M = {}
        LATE = {"lgcp": 0, "aucp": 0, "m6": 0, "apmm": 0, "aumm": 0}
        blk = E(nc.Block())

        @blk.sync
        def _(sy):
            inc(sy.dma_start(out=id_f[:], in_=id_in.ap()), s_c, 16)
            inc(sy.dma_start(out=ones_f[:], in_=ones_in.ap()), s_c, 16)
            inc(sy.dma_start(out=bout_rep[:],
                             in_=bout_in.ap()[0:1, :].partition_broadcast(128)), s_c, 16)
            inc(sy.dma_start(out=nw_f[:], in_=nw_in.ap()), s_c, 16)
            inc(sy.dma_start(out=nb_f[:], in_=nb_in.ap()), s_c, 16)
            inc(sy.dma_start(out=bin_f[:], in_=bin_in.ap()), s_c, 16)
            inc(sy.dma_start(out=ab_f[:], in_=ab_in.ap()), s_c, 16)
            inc(sy.dma_start(out=c_f[:], in_=c_in.ap()), s_c, 16)
            M["consts"] = val(s_c)

        @blk.gpsimd
        def _(g):
            inc(g.dma_start(out=wgbf[:], in_=wg_in.ap()), s_wg, 16)
            w1r = w1_in.ap()[0].rearrange("(dc p) f -> p dc f", p=128)
            inc(g.dma_start(out=w1bf[:, 0:4, :], in_=w1r[:, 0:4, :]), s_w1, 16)
            inc(g.dma_start(out=w1bf[:, 4:8, :], in_=w1r[:, 4:8, :]), s_w1, 16)
            w2r = w2_in.ap()[0].rearrange("(fc p) d -> p fc d", p=128)
            for q in range(4):
                inc(g.dma_start(out=w2bf[:, 4 * q:4 * q + 4, :],
                                in_=w2r[:, 4 * q:4 * q + 4, :]), s_w2, 16)

        @blk.vector
        def _(v):
            v.memset(onec_bf[:], 1.0)
            v.memset(au_tm[:, 16:32], 0.0)
            v.memset(au_tm[:, 48:128], 0.0)
            v.memset(eps_s[:], EPS)
            v.memset(epsk_s[:], EPSK)
            v.memset(hl_s[:], HL1024)

        @blk.scalar
        def _(s):
            s.wait_ge(s_c, M["consts"])
            s.copy(abhi[:], ab_f[:])
            s.copy(chi[:], c_f[:])
            s.copy(binhi[:], bin_f[:])
            inc(s.copy(nbhi[:], nb_f[:]), s_sc)
            M["hi_consts"] = val(s_sc)

        @blk.vector
        def _(v):
            v.wait_ge(s_sc, M["hi_consts"])
            v.tensor_sub(ablo[:], ab_f[:], abhi[:])
            v.tensor_sub(clo[:], c_f[:], chi[:])
            v.tensor_sub(binlo[:], bin_f[:], binhi[:])
            inc(v.tensor_sub(nblo[:], nb_f[:], nbhi[:]), s_v)
            M["lo_consts"] = val(s_v)

        caps = [ps_h[:, 0:32], ps_h[:, 512:544],
                ps_ya[:, 0:32], ps_ya[:, 512:544],
                ps_yb[:, 0:32], ps_yb[:, 512:544],
                ps_ap[:, 0:32], ps_misc[:, 0:32]]
        for ec in range(NEC):
            @blk.sync
            def _(sy, ec=ec):
                if ec >= 2:
                    sy.wait_ge(s_sc, M[("whi", ec - 2)])
                    sy.wait_ge(s_v, M[("wlo", ec - 2)])
                a = sy.dma_start(out=xst[:, ec % 2, :],
                                 in_=win_in.ap()[ec * 128:(ec + 1) * 128, :])
                M[("wld", ec)] = inc(a, s_xs[ec % 2], 16)

            @blk.scalar
            def _(s, ec=ec):
                s.wait_ge(s_xs[ec % 2], M[("wld", ec)])
                if ec >= 1:
                    s.wait_ge(s_t, M[("cabmm", ec - 1)])
                M[("whi", ec)] = inc(s.copy(xnhi_tm[:], xst[:, ec % 2, :]), s_sc)

            @blk.vector
            def _(v, ec=ec):
                v.wait_ge(s_sc, M[("whi", ec)])
                M[("wlo", ec)] = inc(
                    v.tensor_sub(xnlo_tm[:], xst[:, ec % 2, :], xnhi_tm[:]), s_v)

            @blk.tensor
            def _(t, ec=ec):
                t.wait_ge(s_v, M[("wlo", ec)])
                if ec == 0:
                    t.wait_ge(s_sc, M["hi_consts"])
                    t.wait_ge(s_v, M["lo_consts"])
                last = None
                for dc in range(NDC):
                    for p, (lh, rh) in enumerate([
                            (xnhi_tm, abhi), (xnlo_tm, abhi), (xnhi_tm, ablo)]):
                        last = t.matmul(caps[dc], lh[:, dc * 128:(dc + 1) * 128],
                                        rh[:, ec, :],
                                        start=(ec == 0 and p == 0),
                                        stop=(ec == NEC - 1 and p == 2))
                M[("cabmm", ec)] = inc(last, s_t)

        @blk.vector
        def _(v):
            v.wait_ge(s_t, M[("cabmm", NEC - 1)])
            for dc in range(NDC):
                v.tensor_copy(cab_f[:, dc, :], caps[dc])
            v.drain()
            last = None
            for dc in range(NDC):
                last = v.tensor_scalar(cabw_f[:, dc, :], cab_f[:, dc, :],
                                       nw_f[:, dc:dc + 1], None, ALU.mult)
            M["cab_f"] = inc(last, s_v)

        @blk.scalar
        def _(s):
            s.wait_ge(s_v, M["cab_f"])
            s.copy(cabhi[:], cab_f[:])
            M["cabhi"] = inc(s.copy(cabwhi[:], cabw_f[:]), s_sc)

        @blk.vector
        def _(v):
            v.wait_ge(s_sc, M["cabhi"])
            v.tensor_sub(cablo[:], cab_f[:], cabhi[:])
            M["cablo"] = inc(v.tensor_sub(cabwlo[:], cabw_f[:], cabwhi[:]), s_v)

        @blk.tensor
        def _(t):
            t.wait_ge(s_v, M["cablo"])
            for dc in range(NDC):
                for rh in (cabwhi, cabwlo):
                    t.matmul(ps_ap[0:1, :], onec_bf[:], rh[:, dc, :],
                             start=(dc == 0 and rh is cabwhi),
                             stop=(dc == NDC - 1 and rh is cabwlo))
            first = True
            last2 = None
            for dc in range(NDC):
                for lh, rh in [(nbhi, cabhi), (nblo, cabhi), (nbhi, cablo)]:
                    last2 = t.matmul(ps_misc[0:1, 0:32], lh[:, dc:dc + 1],
                                     rh[:, dc, :], start=first, stop=False)
                    first = False
            for ec in range(NEC):
                for p, (lh, rh) in enumerate([(binhi, abhi), (binlo, abhi),
                                              (binhi, ablo)]):
                    last2 = t.matmul(ps_misc[0:1, 0:32], lh[:, ec:ec + 1],
                                     rh[:, ec, :], start=False,
                                     stop=(ec == NEC - 1 and p == 2))
            M["vwcb"] = inc(last2, s_t)

        @blk.vector
        def _(v):
            v.wait_ge(s_t, M["vwcb"])
            v.tensor_copy(vwrow[:], ps_ap[0:1, :])
            M["rows"] = inc(v.tensor_copy(cbrow[:], ps_misc[0:1, 0:32]), s_v)

        @blk.tensor
        def _(t):
            t.wait_ge(s_v, M["rows"])
            M["rep1"] = inc(t.matmul(ps_ap[:], ones_f[:], vwrow[:],
                                     start=True, stop=True), s_t)

        @blk.vector
        def _(v):
            v.wait_ge(s_t, M["rep1"])
            M["vwrep"] = inc(v.tensor_copy(vwrep[:], ps_ap[:]), s_v)

        @blk.tensor
        def _(t):
            t.wait_ge(s_v, M["vwrep"])
            M["rep2"] = inc(t.matmul(ps_ap[:], ones_f[:], cbrow[:],
                                     start=True, stop=True), s_t)

        @blk.vector
        def _(v):
            v.wait_ge(s_t, M["rep2"])
            M["cbrep"] = inc(v.tensor_copy(cbrep[:], ps_ap[:]), s_v)

        for ec in range(NEC):
            @blk.sync
            def _(sy, ec=ec):
                sy.wait_ge(s_sc, M[("mwhi", ec - 2)] if ec >= 2 else M[("whi", NEC - 1)])
                sy.wait_ge(s_v, M[("mwlo", ec - 2)] if ec >= 2 else M[("wlo", NEC - 1)])
                a = sy.dma_start(out=xst[:, ec % 2, :],
                                 in_=woutt_in.ap()[ec * 128:(ec + 1) * 128, :])
                M[("mwld", ec)] = inc(a, s_xs[ec % 2], 16)

            @blk.scalar
            def _(s, ec=ec):
                s.wait_ge(s_xs[ec % 2], M[("mwld", ec)])
                s.wait_ge(s_t, M[("mtmm", ec - 1)] if ec >= 1 else M[("cabmm", NEC - 1)])
                M[("mwhi", ec)] = inc(s.copy(xnhi_tm[:], xst[:, ec % 2, :]), s_sc)

            @blk.vector
            def _(v, ec=ec):
                v.wait_ge(s_sc, M[("mwhi", ec)])
                M[("mwlo", ec)] = inc(
                    v.tensor_sub(xnlo_tm[:], xst[:, ec % 2, :], xnhi_tm[:]), s_v)

            @blk.tensor
            def _(t, ec=ec):
                t.wait_ge(s_v, M[("mwlo", ec)])
                if ec == 0:
                    t.wait_ge(s_v, M["cbrep"])
                last = None
                for hf in range(2):
                    sl = slice(hf * 512, hf * 512 + 512)
                    for p, (lh, rh) in enumerate([(chi, xnhi_tm), (clo, xnhi_tm),
                                                  (chi, xnlo_tm)]):
                        last = t.matmul(ps_yb[0:NS, sl], lh[:, ec, :], rh[:, sl],
                                        start=(ec == 0 and p == 0),
                                        stop=(ec == NEC - 1 and p == 2))
                M[("mtmm", ec)] = inc(last, s_t)

        @blk.vector
        def _(v):
            v.wait_ge(s_t, M[("mtmm", NEC - 1)])
            M["mtbf"] = inc(v.tensor_copy(mtbf[:], ps_yb[0:NS, :]), s_v)

        ge_w1free = {}
        ge_w2free = {}
        xfree = {}
        stores = [0]

        def s1VA(ti, c):
            g = ti * NCH + c
            gp = g % 2
            g3 = g % 3
            g4 = g % 4
            xs = g % 4
            t0 = g * 128

            @blk.sync
            def _(sy):
                if g >= 4:
                    sy.wait_ge(s_v, xfree[g - 4])
                else:
                    sy.wait_ge(s_sc, M[("mwhi", NEC - 1)])
                    sy.wait_ge(s_v, M[("mwlo", NEC - 1)])
                a = sy.dma_start(out=xst[:, xs, :], in_=x_in.ap()[t0:t0 + 128, :])
                M[("x", g)] = inc(a, s_xs[xs], 16)

            @blk.vector
            def _(v):
                v.wait_ge(s_xs[xs], M[("x", g)])
                xt = xst[:, xs, :]
                v.reduce_sum(sum_s[:], xt, AX.X)
                v.drain()
                v.tensor_scalar_mul(mu_s[:], sum_s[:], R1024)
                v.drain()
                v.tensor_scalar_sub(xc_s[:, g3, :], xt, mu_s[:])
                v.drain()
                v.tensor_mul(sq_s[:], xc_s[:, g3, :], xc_s[:, g3, :])
                v.reduce_sum(sxc_s[:, g4:g4 + 1], xc_s[:, g3, :], AX.X)
                v.drain()
                v.reduce_sum(vs_s[:, g3:g3 + 1], sq_s[:], AX.X)
                v.tensor_mul(tq_s[:], sxc_s[:, g4:g4 + 1], sxc_s[:, g4:g4 + 1])
                v.drain()
                v.scalar_tensor_tensor(q_s[:], tq_s[:], -R1024, vs_s[:, g3:g3 + 1],
                                       ALU.mult, ALU.add)
                v.drain()
                v.scalar_tensor_tensor(w_s[:, g3:g3 + 1], vs_s[:, g3:g3 + 1], EPS,
                                       q_s[:], ALU.mult, ALU.add)
                v.drain()
                M[("m1", g)] = inc(
                    v.tensor_scalar_add(w_s[:, g3:g3 + 1], w_s[:, g3:g3 + 1], K2),
                    s_v)

        def s1SB(ti, c):
            g = ti * NCH + c
            g3 = g % 3

            @blk.scalar
            def _(s):
                s.wait_ge(s_v, M[("m1", g)])
                s.activation(u1_s[:], vs_s[:, g3:g3 + 1], AF.Ln, bias=epsk_s[:])
                s.drain()
                s.mul(hu1_s[:], u1_s[:], 0.5)
                s.activation(rs_s[:, g3:g3 + 1], u1_s[:], AF.Exp,
                             scale=-0.5, bias=hl_s[:])
                s.drain()
                s.activation(u2_s[:], w_s[:, g3:g3 + 1], AF.Ln)
                s.drain()
                M[("m2", g)] = inc(
                    s.activation(rs2_c[:, c:c + 1], u2_s[:], AF.Exp,
                                 scale=-0.5, bias=hu1_s[:]), s_sc)

        def s1VC(ti, c):
            g = ti * NCH + c
            gp = g % 2
            g3 = g % 3
            g4 = g % 4
            xs = g % 4
            tb = ti % 2

            @blk.vector
            def _(v):
                v.wait_ge(s_sc, M[("m2", g)])
                if g >= 1:
                    v.wait_ge(s_tps[(g - 1) % 8], M[("tpd", g - 1)])
                else:
                    v.wait_ge(s_t, M[("mtmm", NEC - 1)])
                    v.wait_ge(s_c, M["consts"])
                v.tensor_scalar_mul(xn_tm[:], xc_s[:, g3, :], rs_s[:, g3:g3 + 1])
                v.drain()
                v.tensor_copy(xnhi_tm[:], xn_tm[:])
                v.drain()
                v.tensor_sub(xnlo_tm[:], xn_tm[:], xnhi_tm[:])
                v.tensor_scalar_mul(mu2_s[:], sxc_s[:, g4:g4 + 1], R1024)
                v.drain()
                v.tensor_mul(mu2_s[:], mu2_s[:], rs_s[:, g3:g3 + 1])
                v.drain()
                v.tensor_mul(rm2_c[:, c:c + 1], mu2_s[:], rs2_c[:, c:c + 1])
                v.tensor_add(oacc[:, tb, c, :], xst[:, xs, :], xn_tm[:])
                v.drain()
                ii = v.tensor_add(oacc[:, tb, c, :], oacc[:, tb, c, :], bout_rep[:])
                xfree[g] = inc(ii, s_v)

            @blk.sync
            def _(sy):
                if c == 0 and ti >= 2:
                    sy.wait_ge(s_t, ge_w1free[(ti - 1) * NE])
                sy.wait_ge(s_v, xfree[g])
                stp = s_tps[g % 8]
                for dc in range(NDC):
                    inc(sy.dma_start_transpose(
                        xnhi_fm[:, tb, dc, c * 128:(c + 1) * 128],
                        xnhi_tm[:, dc * 128:(dc + 1) * 128]), stp, 16)
                if g >= 1:
                    sy.wait_ge(s_t, LATE["apmm"])
                for dc in range(NDC):
                    inc(sy.dma_start_transpose(
                        xnlo_fm[:, dc, :],
                        xnlo_tm[:, dc * 128:(dc + 1) * 128]), stp, 16)
                M[("tpd", g)] = val(stp)

        def s1B1(ti, c):
            g = ti * NCH + c
            tb = ti % 2
            pc = c % 2

            @blk.tensor
            def _(t):
                if g == 0:
                    t.wait_ge(s_wg, 16)
                    t.wait_ge(s_sc, M["cabhi"])
                    t.wait_ge(s_v, M["cablo"])
                    t.wait_ge(s_v, M["mtbf"])
                    t.wait_ge(s_v, M["cbrep"])
                if LATE["aucp"]:
                    t.wait_ge(s_v, LATE["aucp"])
                if LATE["lgcp"]:
                    t.wait_ge(s_v, LATE["lgcp"])
                t.wait_ge(s_tps[g % 8], M[("tpd", g)])
                last = None
                for dc in range(NDC):
                    last = t.matmul(ps_lg, xnhi_fm[:, tb, dc, c * 128:(c + 1) * 128],
                                    wgbf[:, dc, :],
                                    start=(dc == 0), stop=(dc == NDC - 1))
                M[("lgmm", g)] = inc(last, s_t)
                if LATE["m6"]:
                    t.wait_ge(s_v, LATE["m6"])
                for p in range(3):
                    for dc in range(NDC):
                        lh = (xnhi_fm[:, tb, dc, c * 128:(c + 1) * 128]
                              if p != 1 else xnlo_fm[:, dc, :])
                        rh = cabwhi[:, dc, :] if p != 2 else cabwlo[:, dc, :]
                        last = t.matmul(ps_ap[:], lh, rh,
                                        start=(p == 0 and dc == 0),
                                        stop=(p == 2 and dc == NDC - 1))
                M[("apmm", g)] = inc(last, s_t)
                LATE["apmm"] = M[("apmm", g)]

            @blk.vector
            def _(v):
                if g >= 2:
                    v.wait_ge(s_sc, M[("m10", g - 2)])
                v.wait_ge(s_t, M[("lgmm", g)])
                M[("lgcp", g)] = inc(v.tensor_copy(lg_sb[:, pc, :], ps_lg), s_v)
                LATE["lgcp"] = M[("lgcp", g)]
                v.wait_ge(s_t, M[("apmm", g)])
                v.scalar_tensor_tensor(term[:], vwrep[:], rm2_c[:, c:c + 1],
                                       cbrep[:], ALU.mult, ALU.subtract)
                v.drain()
                v.scalar_tensor_tensor(apre[:, pc, :], ps_ap[:],
                                        rs2_c[:, c:c + 1], term[:],
                                        ALU.mult, ALU.subtract)
                v.drain()
                M[("m6", g)] = inc(
                    v.tensor_scalar_max(apre[:, pc, 0:NS], apre[:, pc, 0:NS],
                                        -30.0), s_v)
                LATE["m6"] = M[("m6", g)]
                v.reduce_max(m1_s[:, pc:pc + 1], lg_sb[:, pc, :], AX.X)
                v.reduce_max(nm1_s[:, pc:pc + 1], lg_sb[:, pc, :], AX.X, negate=True)
                v.drain()
                v.tensor_scalar(eq_sb[:], lg_sb[:, pc, :], m1_s[:, pc:pc + 1],
                                None, ALU.is_equal)
                v.drain()
                v.scalar_tensor_tensor(msk_sb[:], eq_sb[:], -1e30, lg_sb[:, pc, :],
                                       ALU.mult, ALU.add)
                v.drain()
                M[("m9", g)] = inc(
                    v.reduce_max(m2_s[:, pc:pc + 1], msk_sb[:], AX.X), s_v)

        def s1B2(ti, c):
            g = ti * NCH + c
            tb = ti % 2
            pc = c % 2

            @blk.scalar
            def _(s):
                if g >= 1:
                    s.wait_ge(s_v, M[("router", g - 1)])
                s.wait_ge(s_v, M[("m6", g)])
                M[("m7", g)] = inc(
                    s.activation(sn1[:], apre[:, pc, 0:NS], AF.Exp, scale=-1.0),
                    s_sc)
                s.wait_ge(s_v, M[("m9", g)])
                s.activation(e_sb[:], lg_sb[:, pc, :], AF.Exp,
                             bias=nm1_s[:, pc:pc + 1])
                M[("m10", g)] = inc(
                    s.activation(e2_s[:], m2_s[:, pc:pc + 1], AF.Exp,
                                 bias=nm1_s[:, pc:pc + 1]), s_sc)

            @blk.vector
            def _(v):
                v.wait_ge(s_sc, M[("m7", g)])
                if LATE["aumm"]:
                    v.wait_ge(s_t, LATE["aumm"])
                v.tensor_copy(au_tm[:, 32:48], apre[:, pc, 16:32])
                v.tensor_scalar_add(sn2[:], sn1[:], 1.0)
                v.drain()
                M[("m8", g)] = inc(v.reciprocal(au_tm[:, 0:16], sn2[:]), s_v)
                v.wait_ge(s_sc, M[("m10", g)])
                v.tensor_scalar_add(den_s[:], e2_s[:], 1.0)
                v.drain()
                v.reciprocal(rden_s[:], den_s[:])
                v.tensor_scalar(ind_sb[:], lg_sb[:, pc, :], m2_s[:, pc:pc + 1],
                                None, ALU.is_ge)
                v.drain()
                M[("router", g)] = inc(
                    v.scalar_tensor_tensor(c_sb[:, tb, c, :], ind_sb[:], rden_s[:],
                                           e_sb[:], ALU.mult, ALU.mult), s_v)

            @blk.tensor
            def _(t):
                t.wait_ge(s_v, LATE["lgcp"])
                t.wait_ge(s_v, M[("m8", g)])
                if g == 0:
                    t.wait_ge(s_c, M["consts"])
                M[("aumm", g)] = inc(t.transpose(ps_tr, au_tm[:], id_f[:]), s_t)
                LATE["aumm"] = M[("aumm", g)]

            @blk.vector
            def _(v):
                v.wait_ge(s_t, M[("aumm", g)])
                v.tensor_copy(afm[:, c * 128:(c + 1) * 128], ps_misc[0:NS, 0:128])
                ii = v.tensor_copy(ufm[:, c * 128:(c + 1) * 128],
                                   ps_misc[32:48, 0:128])
                M[("aucp", g)] = inc(ii, s_v)
                LATE["aucp"] = M[("aucp", g)]
                v.drain()

        def scan_emit(ti):
            tb = ti % 2

            @blk.vector
            def _(v):
                if ti % 4 == 0:
                    v.memset(carry_s[:], 0.0)
                v.drain()
                v.tensor_tensor_scan(st_f[:], afm[:], ufm[:],
                                     carry_s[:], ALU.mult, ALU.add)
                v.drain()
                v.tensor_copy(carry_s[:], st_f[:, TT - 1:TT])
                M[("scan", ti)] = inc(v.tensor_copy(st_bf[:, tb, :], st_f[:]), s_v)

        def expert_emit(ti, e):
            ge = ti * NE + e
            tb = ti % 2

            @blk.tensor
            def _(t):
                t.wait_ge(s_w1, 32 * (ge + 1))
                if e == 0:
                    for cc in range(NCH):
                        gg = ti * NCH + cc
                        t.wait_ge(s_tps[gg % 8], M[("tpd", gg)])
                for fc in range(NFC):
                    pp = (fc % 2) * 512
                    rk = 16 * ge + fc - 1
                    if rk > 0:
                        t.wait_ge(s_re, rk)
                    for dc in range(NDC):
                        last = t.matmul(ps_h[:, pp:pp + 512],
                                        w1bf[:, dc, fc * 128:(fc + 1) * 128],
                                        xnhi_fm[:, tb, dc, :],
                                        start=(dc == 0), stop=(dc == NDC - 1))
                    M[("hps", ge, fc)] = inc(last, s_t)
                ge_w1free[ge + 1] = val(s_t)
                t.wait_ge(s_w2, 64 * (ge + 1))
                t.wait_ge(s_re, 16 * ge + 16)
                if e == 0:
                    t.wait_ge(s_v, val(s_v))
                for tc in range(NCH):
                    if 4 * ge + tc - 1 > 0:
                        t.wait_ge(s_ye, 4 * ge + tc - 1)
                    ps = ps_ya if tc % 2 == 0 else ps_yb
                    for hf in range(2):
                        sl = slice(hf * 512, hf * 512 + 512)
                        for fc in range(NFC):
                            last = t.matmul(ps[:, sl],
                                            h_sb[:, fc, tc * 128:(tc + 1) * 128],
                                            w2bf[:, fc, sl],
                                            start=(fc == 0), stop=(fc == NFC - 1))
                    M[("yps", ge, tc)] = inc(last, s_t)
                ge_w2free[ge + 1] = val(s_t)

            @blk.scalar
            def _(s):
                for fc in range(NFC):
                    s.wait_ge(s_t, M[("hps", ge, fc)])
                    pp = (fc % 2) * 512
                    inc(s.activation(h_sb[:, fc, :], ps_h[:, pp:pp + 512],
                                     AF.Relu), s_re)

            @blk.vector
            def _(v):
                for tc in range(NCH):
                    v.wait_ge(s_t, M[("yps", ge, tc)])
                    ps = ps_ya if tc % 2 == 0 else ps_yb
                    inc(v.scalar_tensor_tensor(
                        oacc[:, tb, tc, :], ps[:], c_sb[:, tb, tc, e:e + 1],
                        oacc[:, tb, tc, :], ALU.mult, ALU.add), s_ye)

            @blk.gpsimd
            def _(g):
                nge = ge + 1
                if nge >= NTILE * NE:
                    return
                ne_ = nge % NE
                g.wait_ge(s_t, ge_w1free[nge])
                w1r = w1_in.ap()[ne_].rearrange("(dc p) f -> p dc f", p=128)
                inc(g.dma_start(out=w1bf[:, 0:4, :], in_=w1r[:, 0:4, :]), s_w1, 16)
                inc(g.dma_start(out=w1bf[:, 4:8, :], in_=w1r[:, 4:8, :]), s_w1, 16)
                g.wait_ge(s_t, ge_w2free[nge])
                w2r = w2_in.ap()[ne_].rearrange("(fc p) d -> p fc d", p=128)
                for q in range(4):
                    inc(g.dma_start(out=w2bf[:, 4 * q:4 * q + 4, :],
                                    in_=w2r[:, 4 * q:4 * q + 4, :]), s_w2, 16)

        def final_emit(ti):
            tb = ti % 2
            toff = ti * TT
            for c in range(NCH):
                t0 = toff + c * 128

                @blk.tensor
                def _(t, c=c, t0=t0):
                    t.wait_ge(s_v, M[("scan", ti)])
                    if c < 2:
                        t.wait_ge(s_ye, val(s_ye))
                    else:
                        t.wait_ge(s_v, M[("fin", ti, c - 2)])
                    ps = ps_ya if c % 2 == 0 else ps_yb
                    last = None
                    for hf in range(2):
                        sl = slice(hf * 512, hf * 512 + 512)
                        last = t.matmul(ps[:, sl],
                                        st_bf[:, tb, c * 128:(c + 1) * 128],
                                        mtbf[:, sl], start=True, stop=True)
                    M[("ssm", ti, c)] = inc(last, s_t)

                @blk.vector
                def _(v, c=c):
                    v.wait_ge(s_t, M[("ssm", ti, c)])
                    if ("store", c % 2) in M:
                        v.wait_ge(s_outs[c % 2], M[("store", c % 2)])
                    v.drain()
                    ps = ps_ya if c % 2 == 0 else ps_yb
                    M[("fin", ti, c)] = inc(
                        v.tensor_add(outst[:, c % 2, :], oacc[:, tb, c, :], ps[:]),
                        s_v)

                @blk.sync
                def _(sy, c=c, t0=t0):
                    sy.wait_ge(s_v, M[("fin", ti, c)])
                    M[("store", c % 2)] = inc(
                        sy.dma_start(out=out_d.ap()[t0:t0 + 128, :],
                                     in_=outst[:, c % 2, :]), s_outs[c % 2], 16)
                stores[0] += 1

        for c in range(NCH):
            s1VA(0, c)
            s1SB(0, c)
            s1VC(0, c)
            s1B1(0, c)
            s1B2(0, c)
        scan_emit(0)
        for ti in range(NTILE):
            for e in range(NE):
                expert_emit(ti, e)
                if ti + 1 < NTILE:
                    if 3 <= e <= 6:
                        s1B1(ti + 1, e - 3)
                    if 4 <= e <= 7:
                        s1B2(ti + 1, e - 4)
                    if e <= 3:
                        s1VA(ti + 1, e)
                    if 1 <= e <= 4:
                        s1SB(ti + 1, e - 1)
                    if 2 <= e <= 5:
                        s1VC(ti + 1, e - 2)
                    if e == 7:
                        scan_emit(ti + 1)
            final_emit(ti)

        @blk.sync
        def _(sy):
            sy.wait_ge(s_outs[0], val(s_outs[0]))
            sy.wait_ge(s_outs[1], val(s_outs[1]))

    return nc


_NC_CACHE = [None]


def kernel(**inputs):
    x = np.ascontiguousarray(np.asarray(inputs["L0_attn_x"], dtype=np.float32))
    W1 = np.ascontiguousarray(np.asarray(inputs["L0_ffn_W1"], dtype=np.float32))
    W2 = np.ascontiguousarray(np.asarray(inputs["L0_ffn_W2"], dtype=np.float32))
    Wg = np.asarray(inputs["L0_ffn_Wg"], dtype=np.float32)
    W_in = np.ascontiguousarray(np.asarray(inputs["L0_attn_W_in"], dtype=np.float32))
    A = np.asarray(inputs["L0_attn_A"], dtype=np.float32)
    B = np.asarray(inputs["L0_attn_B_ssm"], dtype=np.float32)
    C = np.asarray(inputs["L0_attn_C_ssm"], dtype=np.float32)
    W_out = np.asarray(inputs["L0_attn_W_out"], dtype=np.float32)
    b_in = np.asarray(inputs["L0_attn_b_in"], dtype=np.float32)
    b_out = np.asarray(inputs["L0_attn_b_out"], dtype=np.float32)
    norm_w = np.asarray(inputs["L0_attn_norm_w"], dtype=np.float32)
    norm_b = np.asarray(inputs["L0_attn_norm_b"], dtype=np.float32)

    AB = np.concatenate([A, B], axis=1)
    ab_l = np.ascontiguousarray(AB.reshape(16, 128, 32).transpose(1, 0, 2))
    c_l = np.ascontiguousarray(C.reshape(16, 128, 16).transpose(1, 0, 2))
    woutt = np.ascontiguousarray(W_out.T)
    wg_l = np.ascontiguousarray(Wg.reshape(8, 128, 8).transpose(1, 0, 2))
    bin_l = np.ascontiguousarray(b_in.reshape(16, 128).T)
    nw_l = np.ascontiguousarray(norm_w.reshape(8, 128).T)
    nb_l = np.ascontiguousarray(norm_b.reshape(8, 128).T)
    bout_l = np.ascontiguousarray(b_out.reshape(1, 1024))
    ones = np.ones((1, 128), np.float32)
    ident = np.eye(128, dtype=np.float32)

    if _NC_CACHE[0] is None:
        _NC_CACHE[0] = _build()
    nc = _NC_CACHE[0]

    in_maps = []
    for i in range(8):
        xi = np.ascontiguousarray(x[2 * i:2 * i + 2].reshape(T, D))
        in_maps.append({
            "x": xi, "w1": W1, "w2": W2, "wg": wg_l, "win": W_in,
            "ab": ab_l, "cm": c_l, "woutt": woutt, "binc": bin_l,
            "nwc": nw_l, "nbc": nb_l, "bout": bout_l, "ones": ones,
            "ident": ident,
        })
    res = run_bass_kernel_spmd(nc, in_maps, core_ids=list(range(8)))
    out = np.stack([np.asarray(r["out"]) for r in res.results])
    return np.ascontiguousarray(out.reshape(16, 2048, 1024)).astype(np.float32, copy=False)


# version 13
# speedup vs baseline: 1.0438x; 1.0341x over previous
import numpy as np
import concourse.bass as bass
import concourse.mybir as mybir
from concourse.bass_utils import run_bass_kernel_spmd
from contextlib import ExitStack

F32 = mybir.dt.float32
BF16 = mybir.dt.bfloat16
AF = mybir.ActivationFunctionType
ALU = mybir.AluOpType
AX = mybir.AxisListType

D = 1024
DI = 2048
NS = 16
T = 4096
TT = 512
NTILE = T // TT
NCH = TT // 128
NE = 8
NDC = D // 128
NFC = DI // 128
NEC = DI // 128
EPS = 1e-5
R1024 = 1.0 / 1024.0
EPSK = 1024.0 * EPS
K2 = 1024.0 * EPS * EPS
HL1024 = 3.4657359027997265


def _build():
    nc = bass.Bass("TRN2", target_bir_lowering=False)

    x_in = nc.dram_tensor("x", [T, D], F32, kind="ExternalInput")
    w1_in = nc.dram_tensor("w1", [NE, D, DI], F32, kind="ExternalInput")
    w2_in = nc.dram_tensor("w2", [NE, DI, D], F32, kind="ExternalInput")
    wg_in = nc.dram_tensor("wg", [128, NDC, NE], F32, kind="ExternalInput")
    win_in = nc.dram_tensor("win", [DI, D], F32, kind="ExternalInput")
    ab_in = nc.dram_tensor("ab", [128, NEC, 32], F32, kind="ExternalInput")
    c_in = nc.dram_tensor("cm", [128, NEC, NS], F32, kind="ExternalInput")
    woutt_in = nc.dram_tensor("woutt", [DI, D], F32, kind="ExternalInput")
    bin_in = nc.dram_tensor("binc", [128, NEC], F32, kind="ExternalInput")
    nw_in = nc.dram_tensor("nwc", [128, NDC], F32, kind="ExternalInput")
    nb_in = nc.dram_tensor("nbc", [128, NDC], F32, kind="ExternalInput")
    bout_in = nc.dram_tensor("bout", [1, D], F32, kind="ExternalInput")
    out_d = nc.dram_tensor("out", [T, D], F32, kind="ExternalOutput")

    cnt = {}

    def inc(ins, sem, n=1):
        ins.then_inc(sem, n)
        k = id(sem)
        cnt[k] = cnt.get(k, 0) + n
        return cnt[k]

    def val(sem):
        return cnt.get(id(sem), 0)

    with ExitStack() as ctx:
        E = ctx.enter_context
        xnhi_fm = E(nc.sbuf_tensor("xnhi_fm", [128, 2, NDC, TT], BF16, is_transpose_dest=True))
        xnlo_fm = E(nc.sbuf_tensor("xnlo_fm", [128, NDC, 128], BF16, is_transpose_dest=True))
        w1bf = E(nc.sbuf_tensor("w1bf", [128, NDC, DI], BF16))
        w2bf = E(nc.sbuf_tensor("w2bf", [128, NFC, D], BF16))
        h_sb = E(nc.sbuf_tensor("h_sb", [128, NFC, TT], BF16))
        oacc = E(nc.sbuf_tensor("oacc", [128, 2, NCH, D], F32))
        xst = E(nc.sbuf_tensor("xst", [128, 4, D], F32))
        outst = E(nc.sbuf_tensor("outst", [128, 2, D], F32))
        xn_tm = E(nc.sbuf_tensor("xn_tm", [128, D], F32))
        xc_s = E(nc.sbuf_tensor("xc_s", [128, 3, D], F32))
        sq_s = E(nc.sbuf_tensor("sq_s", [128, D], F32))
        xnhi_tm = E(nc.sbuf_tensor("xnhi_tm", [128, D], BF16))
        xnlo_tm = E(nc.sbuf_tensor("xnlo_tm", [128, D], BF16))
        auf_a = E(nc.sbuf_tensor("auf_a", [32, TT], F32))
        auf_u = E(nc.sbuf_tensor("auf_u", [32, TT], F32))
        st_f = E(nc.sbuf_tensor("st_f", [NS, TT], F32))
        st_bf = E(nc.sbuf_tensor("st_bf", [NS, 2, TT], BF16))
        carry_s = E(nc.sbuf_tensor("carry_s", [NS, 1], F32))
        au_tm = E(nc.sbuf_tensor("au_tm", [128, 128], F32))
        apre = E(nc.sbuf_tensor("apre", [128, 2, 32], F32))
        term = E(nc.sbuf_tensor("term", [128, 32], F32))
        sn1 = E(nc.sbuf_tensor("sn1", [128, NS], F32))
        sn2 = E(nc.sbuf_tensor("sn2", [128, NS], F32))
        lg_sb = E(nc.sbuf_tensor("lg_sb", [128, 2, NE], F32))
        eq_sb = E(nc.sbuf_tensor("eq_sb", [128, NE], F32))
        msk_sb = E(nc.sbuf_tensor("msk_sb", [128, NE], F32))
        ind_sb = E(nc.sbuf_tensor("ind_sb", [128, NE], F32))
        e_sb = E(nc.sbuf_tensor("e_sb", [128, NE], F32))
        c_sb = E(nc.sbuf_tensor("c_sb", [128, 2, NCH, NE], F32))
        sum_s = E(nc.sbuf_tensor("sum_s", [128, 1], F32))
        mu_s = E(nc.sbuf_tensor("mu_s", [128, 1], F32))
        tq_s = E(nc.sbuf_tensor("tq_s", [128, 1], F32))
        q_s = E(nc.sbuf_tensor("q_s", [128, 1], F32))
        vs_s = E(nc.sbuf_tensor("vs_s", [128, 3], F32))
        sxc_s = E(nc.sbuf_tensor("sxc_s", [128, 4], F32))
        w_s = E(nc.sbuf_tensor("w_s", [128, 3], F32))
        rs_s = E(nc.sbuf_tensor("rs_s", [128, 3], F32))
        u1_s = E(nc.sbuf_tensor("u1_s", [128, 1], F32))
        hu1_s = E(nc.sbuf_tensor("hu1_s", [128, 1], F32))
        u2_s = E(nc.sbuf_tensor("u2_s", [128, 1], F32))
        mu2_s = E(nc.sbuf_tensor("mu2_s", [128, 1], F32))
        rs2_c = E(nc.sbuf_tensor("rs2_c", [128, NCH], F32))
        rm2_c = E(nc.sbuf_tensor("rm2_c", [128, NCH], F32))
        m1_s = E(nc.sbuf_tensor("m1_s", [128, 2], F32))
        nm1_s = E(nc.sbuf_tensor("nm1_s", [128, 2], F32))
        m2_s = E(nc.sbuf_tensor("m2_s", [128, 2], F32))
        e2_s = E(nc.sbuf_tensor("e2_s", [128, 1], F32))
        den_s = E(nc.sbuf_tensor("den_s", [128, 1], F32))
        rden_s = E(nc.sbuf_tensor("rden_s", [128, 1], F32))
        eps_s = E(nc.sbuf_tensor("eps_s", [128, 1], F32))
        epsk_s = E(nc.sbuf_tensor("epsk_s", [128, 1], F32))
        hl_s = E(nc.sbuf_tensor("hl_s", [128, 1], F32))
        wgbf = E(nc.sbuf_tensor("wgbf", [128, NDC, NE], BF16))
        ab_f = E(nc.sbuf_tensor("ab_f", [128, NEC, 32], F32))
        abhi = E(nc.sbuf_tensor("abhi", [128, NEC, 32], BF16))
        ablo = E(nc.sbuf_tensor("ablo", [128, NEC, 32], BF16))
        c_f = E(nc.sbuf_tensor("c_f", [128, NEC, NS], F32))
        chi = E(nc.sbuf_tensor("chi", [128, NEC, NS], BF16))
        clo = E(nc.sbuf_tensor("clo", [128, NEC, NS], BF16))
        bin_f = E(nc.sbuf_tensor("bin_f", [128, NEC], F32))
        binhi = E(nc.sbuf_tensor("binhi", [128, NEC], BF16))
        binlo = E(nc.sbuf_tensor("binlo", [128, NEC], BF16))
        nw_f = E(nc.sbuf_tensor("nw_f", [128, NDC], F32))
        nb_f = E(nc.sbuf_tensor("nb_f", [128, NDC], F32))
        nbhi = E(nc.sbuf_tensor("nbhi", [128, NDC], BF16))
        nblo = E(nc.sbuf_tensor("nblo", [128, NDC], BF16))
        bout_rep = E(nc.sbuf_tensor("bout_rep", [128, D], F32))
        onesr_bf = E(nc.sbuf_tensor("onesr_bf", [1, 128], BF16))
        onec_bf = E(nc.sbuf_tensor("onec_bf", [128, 1], BF16))
        vwhi = E(nc.sbuf_tensor("vwhi", [1, 32], BF16))
        vwlo = E(nc.sbuf_tensor("vwlo", [1, 32], BF16))
        cbhi = E(nc.sbuf_tensor("cbhi", [1, 32], BF16))
        cblo = E(nc.sbuf_tensor("cblo", [1, 32], BF16))
        cab_f = E(nc.sbuf_tensor("cab_f", [128, NDC, 32], F32))
        cabhi = E(nc.sbuf_tensor("cabhi", [128, NDC, 32], BF16))
        cablo = E(nc.sbuf_tensor("cablo", [128, NDC, 32], BF16))
        cabw_f = E(nc.sbuf_tensor("cabw_f", [128, NDC, 32], F32))
        cabwhi = E(nc.sbuf_tensor("cabwhi", [128, NDC, 32], BF16))
        cabwlo = E(nc.sbuf_tensor("cabwlo", [128, NDC, 32], BF16))
        vwrow = E(nc.sbuf_tensor("vwrow", [1, 32], F32))
        cbrow = E(nc.sbuf_tensor("cbrow", [1, 32], F32))
        vwrep = E(nc.sbuf_tensor("vwrep", [128, 32], F32))
        cbrep = E(nc.sbuf_tensor("cbrep", [128, 32], F32))
        mtbf = E(nc.sbuf_tensor("mtbf", [NS, D], BF16))
        ps_h = E(nc.psum_tensor("ps_h", [128, 1024], F32))
        ps_ya = E(nc.psum_tensor("ps_ya", [128, 1024], F32))
        ps_yb = E(nc.psum_tensor("ps_yb", [128, 1024], F32))
        ps_ap = E(nc.psum_tensor("ps_ap", [128, 32], F32))
        ps_misc = E(nc.psum_tensor("ps_misc", [128, 512], F32))
        ps_tr = ps_misc[:, 0:128]
        ps_lg = ps_misc[:, 128:136]
        s_c = E(nc.semaphore("s_c"))
        s_xs = [E(nc.semaphore("s_x%d" % j)) for j in range(4)]
        s_tps = [E(nc.semaphore("s_tp%d" % j)) for j in range(8)]
        s_outs = [E(nc.semaphore("s_out%d" % j)) for j in range(2)]
        s_wg = E(nc.semaphore("s_wg"))
        s_w1 = E(nc.semaphore("s_w1"))
        s_w2 = E(nc.semaphore("s_w2"))
        s_t = E(nc.semaphore("s_t"))
        s_v = E(nc.semaphore("s_v"))
        s_sc = E(nc.semaphore("s_sc"))
        s_ye = E(nc.semaphore("s_ye"))
        s_re = E(nc.semaphore("s_re"))

        M = {}
        LATE = {"lgcp": 0, "m6": 0, "apmm": 0}
        blk = E(nc.Block())

        @blk.sync
        def _(sy):
            inc(sy.dma_start(out=bout_rep[:],
                             in_=bout_in.ap()[0:1, :].partition_broadcast(128)), s_c, 16)
            inc(sy.dma_start(out=nw_f[:], in_=nw_in.ap()), s_c, 16)
            inc(sy.dma_start(out=nb_f[:], in_=nb_in.ap()), s_c, 16)
            inc(sy.dma_start(out=bin_f[:], in_=bin_in.ap()), s_c, 16)
            inc(sy.dma_start(out=ab_f[:], in_=ab_in.ap()), s_c, 16)
            inc(sy.dma_start(out=c_f[:], in_=c_in.ap()), s_c, 16)
            M["consts"] = val(s_c)

        @blk.gpsimd
        def _(g):
            inc(g.dma_start(out=wgbf[:], in_=wg_in.ap()), s_wg, 16)
            w1r = w1_in.ap()[0].rearrange("(dc p) f -> p dc f", p=128)
            inc(g.dma_start(out=w1bf[:, 0:4, :], in_=w1r[:, 0:4, :]), s_w1, 16)
            inc(g.dma_start(out=w1bf[:, 4:8, :], in_=w1r[:, 4:8, :]), s_w1, 16)
            w2r = w2_in.ap()[0].rearrange("(fc p) d -> p fc d", p=128)
            for q in range(4):
                inc(g.dma_start(out=w2bf[:, 4 * q:4 * q + 4, :],
                                in_=w2r[:, 4 * q:4 * q + 4, :]), s_w2, 16)

        @blk.vector
        def _(v):
            v.memset(onec_bf[:], 1.0)
            v.memset(onesr_bf[:], 1.0)
            v.memset(au_tm[:, 16:32], 0.0)
            v.memset(au_tm[:, 48:64], 0.0)
            v.memset(eps_s[:], EPS)
            v.memset(epsk_s[:], EPSK)
            v.memset(hl_s[:], HL1024)

        @blk.scalar
        def _(s):
            s.wait_ge(s_c, M["consts"])
            s.copy(abhi[:], ab_f[:])
            s.copy(chi[:], c_f[:])
            s.copy(binhi[:], bin_f[:])
            inc(s.copy(nbhi[:], nb_f[:]), s_sc)
            M["hi_consts"] = val(s_sc)

        @blk.vector
        def _(v):
            v.wait_ge(s_sc, M["hi_consts"])
            v.tensor_sub(ablo[:], ab_f[:], abhi[:])
            v.tensor_sub(clo[:], c_f[:], chi[:])
            v.tensor_sub(binlo[:], bin_f[:], binhi[:])
            inc(v.tensor_sub(nblo[:], nb_f[:], nbhi[:]), s_v)
            M["lo_consts"] = val(s_v)

        caps = [ps_h[:, 0:32], ps_h[:, 512:544],
                ps_ya[:, 0:32], ps_ya[:, 512:544],
                ps_yb[:, 0:32], ps_yb[:, 512:544],
                ps_ap[:, 0:32], ps_misc[:, 0:32]]
        for ec in range(NEC):
            @blk.sync
            def _(sy, ec=ec):
                if ec >= 2:
                    sy.wait_ge(s_sc, M[("whi", ec - 2)])
                    sy.wait_ge(s_v, M[("wlo", ec - 2)])
                a = sy.dma_start(out=xst[:, ec % 2, :],
                                 in_=win_in.ap()[ec * 128:(ec + 1) * 128, :])
                M[("wld", ec)] = inc(a, s_xs[ec % 2], 16)

            @blk.scalar
            def _(s, ec=ec):
                s.wait_ge(s_xs[ec % 2], M[("wld", ec)])
                if ec >= 1:
                    s.wait_ge(s_t, M[("cabmm", ec - 1)])
                M[("whi", ec)] = inc(s.copy(xnhi_tm[:], xst[:, ec % 2, :]), s_sc)

            @blk.vector
            def _(v, ec=ec):
                v.wait_ge(s_sc, M[("whi", ec)])
                M[("wlo", ec)] = inc(
                    v.tensor_sub(xnlo_tm[:], xst[:, ec % 2, :], xnhi_tm[:]), s_v)

            @blk.tensor
            def _(t, ec=ec):
                t.wait_ge(s_v, M[("wlo", ec)])
                if ec == 0:
                    t.wait_ge(s_sc, M["hi_consts"])
                    t.wait_ge(s_v, M["lo_consts"])
                last = None
                for dc in range(NDC):
                    for p, (lh, rh) in enumerate([
                            (xnhi_tm, abhi), (xnlo_tm, abhi), (xnhi_tm, ablo)]):
                        last = t.matmul(caps[dc], lh[:, dc * 128:(dc + 1) * 128],
                                        rh[:, ec, :],
                                        start=(ec == 0 and p == 0),
                                        stop=(ec == NEC - 1 and p == 2))
                M[("cabmm", ec)] = inc(last, s_t)

        @blk.vector
        def _(v):
            v.wait_ge(s_t, M[("cabmm", NEC - 1)])
            for dc in range(NDC):
                v.tensor_copy(cab_f[:, dc, :], caps[dc])
            v.drain()
            last = None
            for dc in range(NDC):
                last = v.tensor_scalar(cabw_f[:, dc, :], cab_f[:, dc, :],
                                       nw_f[:, dc:dc + 1], None, ALU.mult)
            M["cab_f"] = inc(last, s_v)

        @blk.scalar
        def _(s):
            s.wait_ge(s_v, M["cab_f"])
            s.copy(cabhi[:], cab_f[:])
            M["cabhi"] = inc(s.copy(cabwhi[:], cabw_f[:]), s_sc)

        @blk.vector
        def _(v):
            v.wait_ge(s_sc, M["cabhi"])
            v.tensor_sub(cablo[:], cab_f[:], cabhi[:])
            M["cablo"] = inc(v.tensor_sub(cabwlo[:], cabw_f[:], cabwhi[:]), s_v)

        @blk.tensor
        def _(t):
            t.wait_ge(s_v, M["cablo"])
            for dc in range(NDC):
                for rh in (cabwhi, cabwlo):
                    t.matmul(ps_ap[0:1, :], onec_bf[:], rh[:, dc, :],
                             start=(dc == 0 and rh is cabwhi),
                             stop=(dc == NDC - 1 and rh is cabwlo))
            first = True
            last2 = None
            for dc in range(NDC):
                for lh, rh in [(nbhi, cabhi), (nblo, cabhi), (nbhi, cablo)]:
                    last2 = t.matmul(ps_misc[0:1, 0:32], lh[:, dc:dc + 1],
                                     rh[:, dc, :], start=first, stop=False)
                    first = False
            for ec in range(NEC):
                for p, (lh, rh) in enumerate([(binhi, abhi), (binlo, abhi),
                                              (binhi, ablo)]):
                    last2 = t.matmul(ps_misc[0:1, 0:32], lh[:, ec:ec + 1],
                                     rh[:, ec, :], start=False,
                                     stop=(ec == NEC - 1 and p == 2))
            M["vwcb"] = inc(last2, s_t)

        @blk.vector
        def _(v):
            v.wait_ge(s_t, M["vwcb"])
            v.tensor_copy(vwrow[:], ps_ap[0:1, :])
            M["rows"] = inc(v.tensor_copy(cbrow[:], ps_misc[0:1, 0:32]), s_v)

        @blk.scalar
        def _(s):
            s.wait_ge(s_v, M["rows"])
            s.copy(vwhi[:], vwrow[:])
            M["rephi"] = inc(s.copy(cbhi[:], cbrow[:]), s_sc)

        @blk.vector
        def _(v):
            v.wait_ge(s_sc, M["rephi"])
            v.tensor_sub(vwlo[:], vwrow[:], vwhi[:])
            M["replo"] = inc(v.tensor_sub(cblo[:], cbrow[:], cbhi[:]), s_v)

        @blk.tensor
        def _(t):
            t.wait_ge(s_v, M["replo"])
            t.matmul(ps_ap[:], onesr_bf[:], vwhi[:], start=True, stop=False)
            M["rep1"] = inc(t.matmul(ps_ap[:], onesr_bf[:], vwlo[:],
                                     start=False, stop=True), s_t)

        @blk.vector
        def _(v):
            v.wait_ge(s_t, M["rep1"])
            M["vwrep"] = inc(v.tensor_copy(vwrep[:], ps_ap[:]), s_v)

        @blk.tensor
        def _(t):
            t.wait_ge(s_v, M["vwrep"])
            t.matmul(ps_ap[:], onesr_bf[:], cbhi[:], start=True, stop=False)
            M["rep2"] = inc(t.matmul(ps_ap[:], onesr_bf[:], cblo[:],
                                     start=False, stop=True), s_t)

        @blk.vector
        def _(v):
            v.wait_ge(s_t, M["rep2"])
            M["cbrep"] = inc(v.tensor_copy(cbrep[:], ps_ap[:]), s_v)

        for ec in range(NEC):
            @blk.sync
            def _(sy, ec=ec):
                sy.wait_ge(s_sc, M[("mwhi", ec - 2)] if ec >= 2 else M[("whi", NEC - 1)])
                sy.wait_ge(s_v, M[("mwlo", ec - 2)] if ec >= 2 else M[("wlo", NEC - 1)])
                a = sy.dma_start(out=xst[:, ec % 2, :],
                                 in_=woutt_in.ap()[ec * 128:(ec + 1) * 128, :])
                M[("mwld", ec)] = inc(a, s_xs[ec % 2], 16)

            @blk.scalar
            def _(s, ec=ec):
                s.wait_ge(s_xs[ec % 2], M[("mwld", ec)])
                s.wait_ge(s_t, M[("mtmm", ec - 1)] if ec >= 1 else M[("cabmm", NEC - 1)])
                M[("mwhi", ec)] = inc(s.copy(xnhi_tm[:], xst[:, ec % 2, :]), s_sc)

            @blk.vector
            def _(v, ec=ec):
                v.wait_ge(s_sc, M[("mwhi", ec)])
                M[("mwlo", ec)] = inc(
                    v.tensor_sub(xnlo_tm[:], xst[:, ec % 2, :], xnhi_tm[:]), s_v)

            @blk.tensor
            def _(t, ec=ec):
                t.wait_ge(s_v, M[("mwlo", ec)])
                if ec == 0:
                    t.wait_ge(s_v, M["cbrep"])
                last = None
                for hf in range(2):
                    sl = slice(hf * 512, hf * 512 + 512)
                    for p, (lh, rh) in enumerate([(chi, xnhi_tm), (clo, xnhi_tm),
                                                  (chi, xnlo_tm)]):
                        last = t.matmul(ps_yb[0:NS, sl], lh[:, ec, :], rh[:, sl],
                                        start=(ec == 0 and p == 0),
                                        stop=(ec == NEC - 1 and p == 2))
                M[("mtmm", ec)] = inc(last, s_t)

        @blk.vector
        def _(v):
            v.wait_ge(s_t, M[("mtmm", NEC - 1)])
            M["mtbf"] = inc(v.tensor_copy(mtbf[:], ps_yb[0:NS, :]), s_v)

        ge_w1free = {}
        ge_w2free = {}
        xfree = {}
        stores = [0]

        def s1VA(ti, c):
            g = ti * NCH + c
            gp = g % 2
            g3 = g % 3
            g4 = g % 4
            xs = g % 4
            t0 = g * 128

            @blk.sync
            def _(sy):
                if g >= 4:
                    sy.wait_ge(s_v, xfree[g - 4])
                else:
                    sy.wait_ge(s_sc, M[("mwhi", NEC - 1)])
                    sy.wait_ge(s_v, M[("mwlo", NEC - 1)])
                a = sy.dma_start(out=xst[:, xs, :], in_=x_in.ap()[t0:t0 + 128, :])
                M[("x", g)] = inc(a, s_xs[xs], 16)

            @blk.vector
            def _(v):
                v.wait_ge(s_xs[xs], M[("x", g)])
                xt = xst[:, xs, :]
                v.reduce_sum(sum_s[:], xt, AX.X)
                v.drain()
                v.tensor_scalar_mul(mu_s[:], sum_s[:], R1024)
                v.drain()
                v.tensor_scalar_sub(xc_s[:, g3, :], xt, mu_s[:])
                v.drain()
                v.tensor_mul(sq_s[:], xc_s[:, g3, :], xc_s[:, g3, :])
                v.reduce_sum(sxc_s[:, g4:g4 + 1], xc_s[:, g3, :], AX.X)
                v.drain()
                v.reduce_sum(vs_s[:, g3:g3 + 1], sq_s[:], AX.X)
                v.tensor_mul(tq_s[:], sxc_s[:, g4:g4 + 1], sxc_s[:, g4:g4 + 1])
                v.drain()
                v.scalar_tensor_tensor(q_s[:], tq_s[:], -R1024, vs_s[:, g3:g3 + 1],
                                       ALU.mult, ALU.add)
                v.drain()
                v.scalar_tensor_tensor(w_s[:, g3:g3 + 1], vs_s[:, g3:g3 + 1], EPS,
                                       q_s[:], ALU.mult, ALU.add)
                v.drain()
                M[("m1", g)] = inc(
                    v.tensor_scalar_add(w_s[:, g3:g3 + 1], w_s[:, g3:g3 + 1], K2),
                    s_v)

        def s1SB(ti, c):
            g = ti * NCH + c
            g3 = g % 3

            @blk.scalar
            def _(s):
                s.wait_ge(s_v, M[("m1", g)])
                s.activation(u1_s[:], vs_s[:, g3:g3 + 1], AF.Ln, bias=epsk_s[:])
                s.drain()
                s.mul(hu1_s[:], u1_s[:], 0.5)
                s.activation(rs_s[:, g3:g3 + 1], u1_s[:], AF.Exp,
                             scale=-0.5, bias=hl_s[:])
                s.drain()
                s.activation(u2_s[:], w_s[:, g3:g3 + 1], AF.Ln)
                s.drain()
                M[("m2", g)] = inc(
                    s.activation(rs2_c[:, c:c + 1], u2_s[:], AF.Exp,
                                 scale=-0.5, bias=hu1_s[:]), s_sc)

        def s1VC(ti, c):
            g = ti * NCH + c
            gp = g % 2
            g3 = g % 3
            g4 = g % 4
            xs = g % 4
            tb = ti % 2

            @blk.vector
            def _(v):
                v.wait_ge(s_sc, M[("m2", g)])
                if g >= 1:
                    v.wait_ge(s_tps[(g - 1) % 8], M[("tpd", g - 1)])
                else:
                    v.wait_ge(s_t, M[("mtmm", NEC - 1)])
                    v.wait_ge(s_c, M["consts"])
                v.tensor_scalar_mul(xn_tm[:], xc_s[:, g3, :], rs_s[:, g3:g3 + 1])
                v.drain()
                v.tensor_copy(xnhi_tm[:], xn_tm[:])
                v.drain()
                v.tensor_sub(xnlo_tm[:], xn_tm[:], xnhi_tm[:])
                v.tensor_scalar_mul(mu2_s[:], sxc_s[:, g4:g4 + 1], R1024)
                v.drain()
                v.tensor_mul(mu2_s[:], mu2_s[:], rs_s[:, g3:g3 + 1])
                v.drain()
                v.tensor_mul(rm2_c[:, c:c + 1], mu2_s[:], rs2_c[:, c:c + 1])
                v.tensor_add(oacc[:, tb, c, :], xst[:, xs, :], xn_tm[:])
                v.drain()
                ii = v.tensor_add(oacc[:, tb, c, :], oacc[:, tb, c, :], bout_rep[:])
                xfree[g] = inc(ii, s_v)

            @blk.sync
            def _(sy):
                if c == 0 and ti >= 2:
                    sy.wait_ge(s_t, ge_w1free[(ti - 1) * NE])
                sy.wait_ge(s_v, xfree[g])
                stp = s_tps[g % 8]
                for dc in range(NDC):
                    inc(sy.dma_start_transpose(
                        xnhi_fm[:, tb, dc, c * 128:(c + 1) * 128],
                        xnhi_tm[:, dc * 128:(dc + 1) * 128]), stp, 16)
                if g >= 1:
                    sy.wait_ge(s_t, LATE["apmm"])
                for dc in range(NDC):
                    inc(sy.dma_start_transpose(
                        xnlo_fm[:, dc, :],
                        xnlo_tm[:, dc * 128:(dc + 1) * 128]), stp, 16)
                M[("tpd", g)] = val(stp)

        def s1B1(ti, c):
            g = ti * NCH + c
            tb = ti % 2
            pc = c % 2

            @blk.tensor
            def _(t):
                if g == 0:
                    t.wait_ge(s_wg, 16)
                    t.wait_ge(s_sc, M["cabhi"])
                    t.wait_ge(s_v, M["cablo"])
                    t.wait_ge(s_v, M["mtbf"])
                    t.wait_ge(s_v, M["cbrep"])
                if LATE["lgcp"]:
                    t.wait_ge(s_v, LATE["lgcp"])
                t.wait_ge(s_tps[g % 8], M[("tpd", g)])
                last = None
                for dc in range(NDC):
                    last = t.matmul(ps_lg, xnhi_fm[:, tb, dc, c * 128:(c + 1) * 128],
                                    wgbf[:, dc, :],
                                    start=(dc == 0), stop=(dc == NDC - 1))
                M[("lgmm", g)] = inc(last, s_t)
                if LATE["m6"]:
                    t.wait_ge(s_v, LATE["m6"])
                for p in range(3):
                    for dc in range(NDC):
                        lh = (xnhi_fm[:, tb, dc, c * 128:(c + 1) * 128]
                              if p != 1 else xnlo_fm[:, dc, :])
                        rh = cabwhi[:, dc, :] if p != 2 else cabwlo[:, dc, :]
                        last = t.matmul(ps_ap[:], lh, rh,
                                        start=(p == 0 and dc == 0),
                                        stop=(p == 2 and dc == NDC - 1))
                M[("apmm", g)] = inc(last, s_t)
                LATE["apmm"] = M[("apmm", g)]

            @blk.vector
            def _(v):
                if g >= 2:
                    v.wait_ge(s_sc, M[("m10", g - 2)])
                v.wait_ge(s_t, M[("lgmm", g)])
                M[("lgcp", g)] = inc(v.tensor_copy(lg_sb[:, pc, :], ps_lg), s_v)
                LATE["lgcp"] = M[("lgcp", g)]
                v.wait_ge(s_t, M[("apmm", g)])
                v.scalar_tensor_tensor(term[:], vwrep[:], rm2_c[:, c:c + 1],
                                       cbrep[:], ALU.mult, ALU.subtract)
                v.drain()
                v.scalar_tensor_tensor(apre[:, pc, :], ps_ap[:],
                                        rs2_c[:, c:c + 1], term[:],
                                        ALU.mult, ALU.subtract)
                v.drain()
                M[("m6", g)] = inc(
                    v.tensor_scalar_max(apre[:, pc, 0:NS], apre[:, pc, 0:NS],
                                        -30.0), s_v)
                LATE["m6"] = M[("m6", g)]
                v.reduce_max(m1_s[:, pc:pc + 1], lg_sb[:, pc, :], AX.X)
                v.reduce_max(nm1_s[:, pc:pc + 1], lg_sb[:, pc, :], AX.X, negate=True)
                v.drain()
                v.tensor_scalar(eq_sb[:], lg_sb[:, pc, :], m1_s[:, pc:pc + 1],
                                None, ALU.is_equal)
                v.drain()
                v.scalar_tensor_tensor(msk_sb[:], eq_sb[:], -1e30, lg_sb[:, pc, :],
                                       ALU.mult, ALU.add)
                v.drain()
                M[("m9", g)] = inc(
                    v.reduce_max(m2_s[:, pc:pc + 1], msk_sb[:], AX.X), s_v)

        def s1B2(ti, c):
            g = ti * NCH + c
            tb = ti % 2
            pc = c % 2

            @blk.scalar
            def _(s):
                if g >= 1:
                    s.wait_ge(s_v, M[("router", g - 1)])
                s.wait_ge(s_v, M[("m6", g)])
                M[("m7", g)] = inc(
                    s.activation(sn1[:], apre[:, pc, 0:NS], AF.Exp, scale=-1.0),
                    s_sc)
                s.wait_ge(s_v, M[("m9", g)])
                s.activation(e_sb[:], lg_sb[:, pc, :], AF.Exp,
                             bias=nm1_s[:, pc:pc + 1])
                M[("m10", g)] = inc(
                    s.activation(e2_s[:], m2_s[:, pc:pc + 1], AF.Exp,
                                 bias=nm1_s[:, pc:pc + 1]), s_sc)

            @blk.vector
            def _(v):
                v.wait_ge(s_sc, M[("m7", g)])
                v.tensor_copy(au_tm[:, 32:48], apre[:, pc, 16:32])
                v.tensor_scalar_add(sn2[:], sn1[:], 1.0)
                v.drain()
                v.reciprocal(au_tm[:, 0:16], sn2[:])
                v.drain()
                for rb in range(4):
                    for dst, cbk in ((auf_a, 0), (auf_u, 1)):
                        v.transpose(
                            dst[0:32, c * 128 + 32 * rb:c * 128 + 32 * rb + 32],
                            au_tm[32 * rb:32 * rb + 32, 32 * cbk:32 * cbk + 32])
                v.drain()
                v.wait_ge(s_sc, M[("m10", g)])
                v.tensor_scalar_add(den_s[:], e2_s[:], 1.0)
                v.drain()
                v.reciprocal(rden_s[:], den_s[:])
                v.tensor_scalar(ind_sb[:], lg_sb[:, pc, :], m2_s[:, pc:pc + 1],
                                None, ALU.is_ge)
                v.drain()
                M[("router", g)] = inc(
                    v.scalar_tensor_tensor(c_sb[:, tb, c, :], ind_sb[:], rden_s[:],
                                           e_sb[:], ALU.mult, ALU.mult), s_v)

        def scan_emit(ti):
            tb = ti % 2

            @blk.vector
            def _(v):
                if ti % 4 == 0:
                    v.memset(carry_s[:], 0.0)
                v.drain()
                v.tensor_tensor_scan(st_f[:], auf_a[0:NS, :], auf_u[0:NS, :],
                                     carry_s[:], ALU.mult, ALU.add)
                v.drain()
                v.tensor_copy(carry_s[:], st_f[:, TT - 1:TT])
                M[("scan", ti)] = inc(v.tensor_copy(st_bf[:, tb, :], st_f[:]), s_v)

        def expert_emit(ti, e):
            ge = ti * NE + e
            tb = ti % 2

            @blk.tensor
            def _(t):
                t.wait_ge(s_w1, 32 * (ge + 1))
                if e == 0:
                    for cc in range(NCH):
                        gg = ti * NCH + cc
                        t.wait_ge(s_tps[gg % 8], M[("tpd", gg)])
                for fc in range(NFC):
                    pp = (fc % 2) * 512
                    rk = 16 * ge + fc - 1
                    if rk > 0:
                        t.wait_ge(s_re, rk)
                    for dc in range(NDC):
                        last = t.matmul(ps_h[:, pp:pp + 512],
                                        w1bf[:, dc, fc * 128:(fc + 1) * 128],
                                        xnhi_fm[:, tb, dc, :],
                                        start=(dc == 0), stop=(dc == NDC - 1))
                    M[("hps", ge, fc)] = inc(last, s_t)
                ge_w1free[ge + 1] = val(s_t)
                t.wait_ge(s_w2, 64 * (ge + 1))
                t.wait_ge(s_re, 16 * ge + 16)
                if e == 0:
                    t.wait_ge(s_v, val(s_v))
                for tc in range(NCH):
                    if 4 * ge + tc - 1 > 0:
                        t.wait_ge(s_ye, 4 * ge + tc - 1)
                    ps = ps_ya if tc % 2 == 0 else ps_yb
                    for hf in range(2):
                        sl = slice(hf * 512, hf * 512 + 512)
                        for fc in range(NFC):
                            last = t.matmul(ps[:, sl],
                                            h_sb[:, fc, tc * 128:(tc + 1) * 128],
                                            w2bf[:, fc, sl],
                                            start=(fc == 0), stop=(fc == NFC - 1))
                    M[("yps", ge, tc)] = inc(last, s_t)
                ge_w2free[ge + 1] = val(s_t)

            @blk.scalar
            def _(s):
                for fc in range(NFC):
                    s.wait_ge(s_t, M[("hps", ge, fc)])
                    pp = (fc % 2) * 512
                    inc(s.activation(h_sb[:, fc, :], ps_h[:, pp:pp + 512],
                                     AF.Relu), s_re)

            @blk.vector
            def _(v):
                for tc in range(NCH):
                    v.wait_ge(s_t, M[("yps", ge, tc)])
                    ps = ps_ya if tc % 2 == 0 else ps_yb
                    inc(v.scalar_tensor_tensor(
                        oacc[:, tb, tc, :], ps[:], c_sb[:, tb, tc, e:e + 1],
                        oacc[:, tb, tc, :], ALU.mult, ALU.add), s_ye)

            @blk.gpsimd
            def _(g):
                nge = ge + 1
                if nge >= NTILE * NE:
                    return
                ne_ = nge % NE
                g.wait_ge(s_t, ge_w1free[nge])
                w1r = w1_in.ap()[ne_].rearrange("(dc p) f -> p dc f", p=128)
                inc(g.dma_start(out=w1bf[:, 0:4, :], in_=w1r[:, 0:4, :]), s_w1, 16)
                inc(g.dma_start(out=w1bf[:, 4:8, :], in_=w1r[:, 4:8, :]), s_w1, 16)
                g.wait_ge(s_t, ge_w2free[nge])
                w2r = w2_in.ap()[ne_].rearrange("(fc p) d -> p fc d", p=128)
                for q in range(4):
                    inc(g.dma_start(out=w2bf[:, 4 * q:4 * q + 4, :],
                                    in_=w2r[:, 4 * q:4 * q + 4, :]), s_w2, 16)

        def final_emit(ti):
            tb = ti % 2
            toff = ti * TT
            for c in range(NCH):
                t0 = toff + c * 128

                @blk.tensor
                def _(t, c=c, t0=t0):
                    t.wait_ge(s_v, M[("scan", ti)])
                    if c < 2:
                        t.wait_ge(s_ye, val(s_ye))
                    else:
                        t.wait_ge(s_v, M[("fin", ti, c - 2)])
                    ps = ps_ya if c % 2 == 0 else ps_yb
                    last = None
                    for hf in range(2):
                        sl = slice(hf * 512, hf * 512 + 512)
                        last = t.matmul(ps[:, sl],
                                        st_bf[:, tb, c * 128:(c + 1) * 128],
                                        mtbf[:, sl], start=True, stop=True)
                    M[("ssm", ti, c)] = inc(last, s_t)

                @blk.vector
                def _(v, c=c):
                    v.wait_ge(s_t, M[("ssm", ti, c)])
                    if ("store", c % 2) in M:
                        v.wait_ge(s_outs[c % 2], M[("store", c % 2)])
                    v.drain()
                    ps = ps_ya if c % 2 == 0 else ps_yb
                    M[("fin", ti, c)] = inc(
                        v.tensor_add(outst[:, c % 2, :], oacc[:, tb, c, :], ps[:]),
                        s_v)

                @blk.sync
                def _(sy, c=c, t0=t0):
                    sy.wait_ge(s_v, M[("fin", ti, c)])
                    M[("store", c % 2)] = inc(
                        sy.dma_start(out=out_d.ap()[t0:t0 + 128, :],
                                     in_=outst[:, c % 2, :]), s_outs[c % 2], 16)
                stores[0] += 1

        for c in range(NCH):
            s1VA(0, c)
            s1SB(0, c)
            s1VC(0, c)
            s1B1(0, c)
            s1B2(0, c)
        scan_emit(0)
        for ti in range(NTILE):
            for e in range(NE):
                expert_emit(ti, e)
                if ti + 1 < NTILE:
                    if 3 <= e <= 6:
                        s1B1(ti + 1, e - 3)
                    if 4 <= e <= 7:
                        s1B2(ti + 1, e - 4)
                    if e <= 3:
                        s1VA(ti + 1, e)
                    if 1 <= e <= 4:
                        s1SB(ti + 1, e - 1)
                    if 2 <= e <= 5:
                        s1VC(ti + 1, e - 2)
                    if e == 7:
                        scan_emit(ti + 1)
            final_emit(ti)

        @blk.sync
        def _(sy):
            sy.wait_ge(s_outs[0], val(s_outs[0]))
            sy.wait_ge(s_outs[1], val(s_outs[1]))

    return nc


_NC_CACHE = [None]


def kernel(**inputs):
    x = np.ascontiguousarray(np.asarray(inputs["L0_attn_x"], dtype=np.float32))
    W1 = np.ascontiguousarray(np.asarray(inputs["L0_ffn_W1"], dtype=np.float32))
    W2 = np.ascontiguousarray(np.asarray(inputs["L0_ffn_W2"], dtype=np.float32))
    Wg = np.asarray(inputs["L0_ffn_Wg"], dtype=np.float32)
    W_in = np.ascontiguousarray(np.asarray(inputs["L0_attn_W_in"], dtype=np.float32))
    A = np.asarray(inputs["L0_attn_A"], dtype=np.float32)
    B = np.asarray(inputs["L0_attn_B_ssm"], dtype=np.float32)
    C = np.asarray(inputs["L0_attn_C_ssm"], dtype=np.float32)
    W_out = np.asarray(inputs["L0_attn_W_out"], dtype=np.float32)
    b_in = np.asarray(inputs["L0_attn_b_in"], dtype=np.float32)
    b_out = np.asarray(inputs["L0_attn_b_out"], dtype=np.float32)
    norm_w = np.asarray(inputs["L0_attn_norm_w"], dtype=np.float32)
    norm_b = np.asarray(inputs["L0_attn_norm_b"], dtype=np.float32)

    AB = np.concatenate([A, B], axis=1)
    ab_l = np.ascontiguousarray(AB.reshape(16, 128, 32).transpose(1, 0, 2))
    c_l = np.ascontiguousarray(C.reshape(16, 128, 16).transpose(1, 0, 2))
    woutt = np.ascontiguousarray(W_out.T)
    wg_l = np.ascontiguousarray(Wg.reshape(8, 128, 8).transpose(1, 0, 2))
    bin_l = np.ascontiguousarray(b_in.reshape(16, 128).T)
    nw_l = np.ascontiguousarray(norm_w.reshape(8, 128).T)
    nb_l = np.ascontiguousarray(norm_b.reshape(8, 128).T)
    bout_l = np.ascontiguousarray(b_out.reshape(1, 1024))

    if _NC_CACHE[0] is None:
        _NC_CACHE[0] = _build()
    nc = _NC_CACHE[0]

    in_maps = []
    for i in range(8):
        xi = np.ascontiguousarray(x[2 * i:2 * i + 2].reshape(T, D))
        in_maps.append({
            "x": xi, "w1": W1, "w2": W2, "wg": wg_l, "win": W_in,
            "ab": ab_l, "cm": c_l, "woutt": woutt, "binc": bin_l,
            "nwc": nw_l, "nbc": nb_l, "bout": bout_l,
        })
    res = run_bass_kernel_spmd(nc, in_maps, core_ids=list(range(8)))
    out = np.stack([np.asarray(r["out"]) for r in res.results])
    return np.ascontiguousarray(out.reshape(16, 2048, 1024)).astype(np.float32, copy=False)
